# version 12
# baseline (speedup 1.0000x reference)
import numpy as np

import concourse.bass as bass
import concourse.mybir as mybir
from concourse.bass_utils import run_bass_kernel_spmd

F32 = mybir.dt.float32
F32R = mybir.dt.float32r
BF16 = mybir.dt.bfloat16
AF = mybir.ActivationFunctionType
OP = mybir.AluOpType
AX = mybir.AxisListType.X

S, D, NT, NDC, H, I, NB = 2048, 2048, 16, 16, 16, 8192, 64
STRIP, NSTRIP = 512, 4
EXP_BIAS = -12.0
QK_SCALE = 1.0 / 128.0 ** 0.5

ENGINES = ("pe", "act", "dve", "sp", "gp")
NCH = {"sp": 7, "gp": 5, "sc": 4}
QSEMS = tuple(f"q_{q}{i}" for q in NCH for i in range(NCH[q]))
SEMS = ("pe", "act", "dve", "gp_e") + QSEMS
OWN = {"pe": "pe", "act": "act", "dve": "dve"}


OWN_ALL = {"pe": "pe", "act": "act", "dve": "dve", "gp": "gp_e"}


class B:
    def __init__(self, nc, debug_sync=False):
        self.nc = nc
        self.debug_sync = debug_sync
        self.plan = {e: [] for e in ENGINES}
        self.cnt = {s: 0 for s in SEMS}
        self.sems = {}
        self.bank_ready = [0] * 8
        self.bank_free = [[] for _ in range(8)]
        self.qrr = {"sp": 0, "gp": 0, "sc": 0}

    def emit(self, eng, fn):
        self.plan[eng].append(fn)

    def wait(self, eng, sem, val):
        if val is None or val <= 0:
            return
        if OWN.get(eng) == sem:
            assert val <= self.cnt[sem], "self-wait on future value would hang"
            return
        sems = self.sems

        def f(e, sem=sem, val=val):
            e.wait_ge(sems[sem], val)
        self.emit(eng, f)

    AUTO_SELF = ("dve", "act", "gp")

    def op(self, eng, fn, sem=None, n=1):
        if (self.debug_sync and eng in OWN_ALL) or eng in self.AUTO_SELF:
            own = OWN_ALL[eng]
            val = self.cnt[own]
            if val > 0:
                sems = self.sems
                self.emit(eng, lambda e, own=own, val=val:
                          e.wait_ge(sems[own], val))
        if sem is None:
            self.emit(eng, lambda e, fn=fn: fn(e))
            return None
        self.cnt[sem] += n
        tgt = self.cnt[sem]
        sems = self.sems

        def f(e, fn=fn, sem=sem, n=n):
            fn(e).then_inc(sems[sem], n)
        self.emit(eng, f)
        return tgt

    def dma(self, queue, out_ap, in_ap, transpose=False, slow_ok=False):
        eng = {"sp": "sp", "gp": "gp", "sc": "act"}[queue]
        ch = self.qrr[queue] % NCH[queue]
        self.qrr[queue] += 1
        sem = f"q_{queue}{ch}"
        self.wait(eng, sem, self.cnt[sem])
        if transpose:
            fn = lambda e, o=out_ap, i=in_ap: e.dma_start(o, i, transpose=True)
        elif slow_ok:
            fn = lambda e, o=out_ap, i=in_ap: e.dma_start(
                o, i, allow_slow_non_contiguous=True)
        else:
            fn = lambda e, o=out_ap, i=in_ap: e.dma_start(o, i)
        val = self.op(eng, fn, sem, 16)
        return (sem, val)

    def wait_tok(self, eng, tok):
        if tok is None or tok == 0:
            return
        if isinstance(tok, tuple):
            self.wait(eng, tok[0], tok[1])
        else:
            raise TypeError(tok)

    def wait_toks(self, eng, toks):
        best = {}
        for t in toks:
            if t is None or t == 0:
                continue
            best[t[0]] = max(best.get(t[0], 0), t[1])
        for sem, val in sorted(best.items()):
            self.wait(eng, sem, val)

    def bank_acquire(self, i):
        for (sem, val) in self.bank_free[i]:
            self.wait("pe", sem, val)
        self.bank_free[i] = []

    def bank_written(self, i):
        self.bank_ready[i] = self.cnt["pe"]

    def bank_read(self, eng, i):
        self.wait(eng, "pe", self.bank_ready[i])

    def bank_read_done(self, eng, i):
        self.bank_free[i].append((OWN[eng], self.cnt[OWN[eng]]))

    def wait_all(self, eng):
        for s in SEMS:
            self.wait(eng, s, self.cnt[s])


def build_nc(phases=4, taps=(), debug_sync=False):
    nc = bass.Bass("TRN2", target_bir_lowering=False,
                   detect_race_conditions=debug_sync)
    b = B(nc, debug_sync=debug_sync)

    x_hbm = nc.dram_tensor("x", [S, D], F32, kind="ExternalInput").ap()
    wqkv_h = nc.dram_tensor("w_qkv", [D, 3 * D], F32, kind="ExternalInput").ap()
    wout_h = nc.dram_tensor("w_out", [D, D], F32, kind="ExternalInput").ap()
    wg_h = nc.dram_tensor("gate_w", [I, D], F32, kind="ExternalInput").ap()
    wu_h = nc.dram_tensor("up_w", [I, D], F32, kind="ExternalInput").ap()
    wd_h = nc.dram_tensor("down_w", [D, I], F32, kind="ExternalInput").ap()
    r1_h = nc.dram_tensor("router_w1", [128, D], F32, kind="ExternalInput").ap()
    r2_h = nc.dram_tensor("router_w2", [NB, 128], F32, kind="ExternalInput").ap()
    c_h = nc.dram_tensor("c", [1, D], F32, kind="ExternalInput").ap()
    wmod_h = nc.dram_tensor("wmod", [D, 6 * D], F32, kind="ExternalInput").ap()
    y_hbm = nc.dram_tensor("y", [S, D], F32, kind="ExternalOutput").ap()

    wqkv_bf = nc.dram_tensor("wqkv_bf", [D, 3 * D], BF16, kind="Internal").ap()
    wout_bf = nc.dram_tensor("wout_bf", [D, D], BF16, kind="Internal").ap()
    wg_bf = nc.dram_tensor("wg_bf", [I, D], BF16, kind="Internal").ap()
    wu_bf = nc.dram_tensor("wu_bf", [I, D], BF16, kind="Internal").ap()
    wd_bf = nc.dram_tensor("wd_bf", [D, I], BF16, kind="Internal").ap()
    wgT_h = nc.dram_tensor("wgT", [NB * 128 * NDC, 128], BF16, kind="Internal").ap()
    wuT_h = nc.dram_tensor("wuT", [NB * 128 * NDC, 128], BF16, kind="Internal").ap()
    wdT_h = nc.dram_tensor("wdT", [4 * NB * 128 * 4, 128], BF16, kind="Internal").ap()
    vec_scr = nc.dram_tensor("vec_scr", [2, D], F32, kind="Internal").ap()

    tap_aps = {}
    for tname, shape, dt in taps:
        tap_aps[tname] = nc.dram_tensor("tap_" + tname, list(shape), dt,
                                        kind="ExternalOutput").ap()

    alloc = nc.alloc_sbuf_tensor
    ones_f = alloc("ones_f", [1, 128], F32).ap()
    ones_bf = alloc("ones_bf", [1, 128], BF16).ap()
    ident = alloc("ident", [128, 128], BF16).ap()
    id64b = alloc("id64b", [64, 64], BF16).ap()
    id64f = alloc("id64f", [64, 64], F32).ap()
    c_sb = alloc("c_sb", [128, 16], F32).ap()
    silu_c = alloc("silu_c", [128, 16], F32).ap()
    silu_cb = alloc("silu_cb", [128, 16], BF16).ap()
    ga_cols = alloc("ga_cols", [128, 16], F32).ap()
    gb_cols = alloc("gb_cols", [128, 16], F32).ap()
    stats = alloc("stats", [128, 64], F32).ap()
    stats2 = alloc("stats2", [128, 64], F32).ap()
    lnscr = alloc("lnscr", [128, 32], F32).ap()
    lnsc2 = alloc("lnsc2", [128, 16], F32).ap()
    nmr2 = alloc("nmr2", [128, 16], F32).ap()
    inv4 = alloc("inv4", [128, 4], F32).ap()
    cb1_rep = alloc("cb1_rep", [128, D], BF16).ap()
    sb_rep = alloc("sb_rep", [128, D], BF16).ap()
    r1T_hi = alloc("r1T_hi", [128, D], BF16).ap()
    r1T_lo = alloc("r1T_lo", [128, D], BF16).ap()
    r2T = alloc("r2T", [128, NB], F32).ap()
    r2_sb = alloc("r2_sb", [64, 128], F32).ap()
    logits = alloc("logits", [128, 4 * NB], F32).ap()
    logits_z = alloc("logits_z", [128, 4 * NB], F32).ap()
    logits_z2 = alloc("logits_z2", [128, 4 * NB], F32).ap()
    m8 = alloc("m8", [128, 8], F32).ap()
    mask_tm = alloc("mask_tm", [128, 4 * NB], BF16).ap()
    mask_fm = alloc("mask_fm", [64, STRIP], BF16).ap()
    dout_tile = alloc("dout_tile", [128, 512], BF16).ap()
    silu_sb = alloc("silu_sb", [128, 512], BF16).ap()
    cexp = alloc("cexp", [128, 2], F32).ap()

    R1 = alloc("R1", [128, 32768], BF16).ap()
    R2 = alloc("R2", [128, 32768], BF16).ap()
    A = alloc("A", [128, 24576], BF16).ap()

    PS = nc.alloc_psum_tensor("PS", [128, 4096], F32).ap()

    def bank(i):
        return PS[:, 512 * i:512 * (i + 1)]

    def bank4(i):
        return bank(i).rearrange("p (a c) -> p a c", a=4)

    def bankb(i):
        return bank(i).bitcast(BF16)

    def bank4b(i):
        return bankb(i)[:, 0:512].rearrange("p (a c) -> p a c", a=4)

    def fv(base, el, n, parts=128):
        assert el % 2 == 0 and el + 2 * n <= base.shape[1]
        return base[0:parts, el:el + 2 * n].bitcast(F32)

    xbuf = fv(R2, 16384, 8192)
    sq_dummy = R2[:, 12288:14336]
    h_tm = R2[:, 14336:16384]
    ln_u = fv(R2, 8192, 2048)
    R2v = R2.rearrange("p (c t) -> p c t", c=16)

    PT = A[:, 0:4096]
    xre = (fv(A, 0, 2048), fv(A, 4096, 2048))
    q_sb = A[:, 4096:6144]
    k_sb = A[:, 6144:8192]
    v2_sb = A[:, 8192:12320]
    v2r = v2_sb.rearrange("p (t h j) -> p t h j", t=16, h=2)
    wq_t = A[:, 12320:14368]
    wk_t = A[:, 14368:16416]
    wv_t = A[:, 16416:20512]
    prep_buf = A[:, 20512:22560]
    wdf_in = (fv(A, 20512, 512), fv(A, 21536, 512))
    wdf_out = (A[:, 22560:23072], A[:, 23072:23584])
    r1_f32 = fv(A, 8192, 2048)
    sq_d2 = A[:, 8192:10240]
    x1_scr = A[:, 10240:12288]
    h2_hi = A[:, 0:8192]
    h2_lo = A[:, 8192:16384]
    wring = A[:, 8192:16384]
    dring = A[:, 8192:16384]
    f32buf = fv(A, 16384, 1024)
    hh = A[:, 18432:19456]
    hl = A[:, 19456:20480]
    rrbuf = fv(A, 20512, 512)
    f32buf_b = fv(A, 21536, 1024)
    wo_sb = A[:, 12320:16416]
    out_stage = fv(A, 0, 2048)
    mrow = A[0:1, 18432:19968]

    R1v = R1.rearrange("p (c t) -> p c t", c=16)
    hidv = R1.rearrange("p (c j) -> p c j", c=NB)

    sa_repA = fv(A, 0, 2048)
    ca1_repA = fv(A, 4096, 2048)

    b.op("dve", lambda e: e.memset(ones_f, 1.0), "dve")
    b.op("dve", lambda e: e.memset(ones_bf, 1.0), "dve")
    b.op("dve", lambda e: e.memset(cexp[:, 0:1], EXP_BIAS), "dve")
    t_consts = b.op("dve", lambda e: e.memset(cexp[:, 1:2], QK_SCALE), "dve")
    t_ms = b.op("gp", lambda e: e.memset(ident, 0.0), "gp_e")
    b.wait("gp", "gp_e", t_ms)
    b.op("gp", lambda e: e.affine_select(out=ident, in_=ident,
         compare_op=OP.not_equal, fill=1.0, base=0, pattern=[[-1, 128]],
         channel_multiplier=1), "gp_e")
    for idt in (id64b, id64f):
        t_ms = b.op("gp", lambda e, t=idt: e.memset(t, 0.0), "gp_e")
        b.wait("gp", "gp_e", t_ms)
        b.op("gp", lambda e, t=idt: e.affine_select(out=t, in_=t,
             compare_op=OP.not_equal, fill=1.0, base=0, pattern=[[-1, 64]],
             channel_multiplier=1), "gp_e")
    b.op("dve", lambda e: e.memset(v2r[:, :, :, 128:129], 1.0), "dve")

    t = b.dma("gp", c_sb, c_h.rearrange("a (k p) -> (a p) k", p=128), slow_ok=True)
    b.wait_tok("act", t)
    b.wait("act", "dve", t_consts)
    b.op("act", lambda e: e.activation(silu_c, c_sb, AF.Silu), "act")
    t_siluc = b.op("act", lambda e: e.activation(silu_cb, c_sb, AF.Silu), "act")

    slab_tok = {}
    slab_mm = {}

    def ring_slot(idx):
        return R2[:, (idx % 2) * 2048:(idx % 2) * 2048 + 2048]

    def mod_slab_dma(idx):
        if idx >= 2:
            b.wait("gp", "pe", slab_mm[idx - 2])
        g, kc = divmod(idx, 16)
        slab_tok[idx] = b.dma("gp", ring_slot(idx),
                              wmod_h[kc * 128:(kc + 1) * 128,
                                     g * 2048:(g + 1) * 2048])

    def mod_slab_mms(idx):
        g, kc = divmod(idx, 16)
        slot = ring_slot(idx)
        b.wait_tok("pe", slab_tok[idx])
        if idx == 0:
            b.wait("pe", "act", t_siluc)
        for q in range(4):
            if kc == 0:
                b.bank_acquire(q)
            b.op("pe", lambda e, o=bank(q)[0:1, :], l=silu_cb[:, kc:kc + 1],
                 r=slot[:, q * 512:(q + 1) * 512], s=(kc == 0), p=(kc == 15):
                 e.matmul(o, l, r, start=s, stop=p), "pe")
            if kc == 15:
                b.bank_written(q)
        slab_mm[idx] = b.cnt["pe"]

    bounce = [fv(R2, 4096 + 1024 * j, 512)[0:1, :] for j in range(4)]

    def repl_from_banks(dst_ap, plus1, pe_guard, dma_guard=None):
        if pe_guard:
            b.wait("dve", "pe", pe_guard)
        if dma_guard is not None:
            b.wait_tok("dve", dma_guard)
        cp_t = [0] * 4
        for q in range(4):
            b.bank_read("dve", q)
            cp_t[q] = b.op("dve", lambda e, d=bounce[q], s=bank(q)[0:1, :]:
                           e.tensor_copy(d, s), "dve")
            b.bank_read_done("dve", q)
        for q in range(4):
            bk = 6 + q % 2
            b.bank_acquire(bk)
            b.wait("pe", "dve", cp_t[q])
            b.op("pe", lambda e, o=bank(bk), l=ones_f, r=bounce[q]:
                 e.matmul(o, l, r, start=True, stop=True), "pe")
            b.bank_written(bk)
            b.bank_read("dve", bk)
            if plus1:
                b.op("dve", lambda e, d=dst_ap[:, q * 512:(q + 1) * 512],
                     s=bank(bk): e.tensor_scalar(d, s, 1.0, None, OP.add), "dve")
            else:
                b.op("dve", lambda e, d=dst_ap[:, q * 512:(q + 1) * 512],
                     s=bank(bk): e.tensor_copy(d, s), "dve")
            b.bank_read_done("dve", bk)
        return b.cnt["pe"]

    def gagb_post(row, pe_guard):
        sgn_scr, sgn2, abs_scr, abs2 = bounce
        if pe_guard:
            b.wait("dve", "pe", pe_guard)
            b.wait("act", "pe", pe_guard)
        toks = []
        for q in range(4):
            b.bank_read("dve", q)
            b.bank_read("act", q)
            src = bank(q)[0:1, :]
            b.op("dve", lambda e, s=src: e.tensor_scalar(
                sgn_scr, s, 0.0, 2.0, OP.is_ge, OP.mult), "dve")
            b.op("dve", lambda e: e.tensor_scalar(
                sgn2, sgn_scr, -1.0, None, OP.add), "dve")
            b.wait("act", "dve", b.cnt["dve"])
            if q > 0:
                b.wait_tok("act", toks[q - 1])
            b.op("act", lambda e, s=src: e.activation(abs_scr, s, AF.Abs), "act")
            b.wait("dve", "act", b.cnt["act"])
            b.op("dve", lambda e: e.tensor_scalar(abs2, abs_scr, 0.25, None,
                                                  OP.max), "dve")
            b.op("dve", lambda e: e.tensor_mul(abs_scr, sgn2, abs2), "dve")
            b.wait("act", "dve", b.cnt["dve"])
            toks.append(b.dma("sc", vec_scr[row:row + 1, q * 512:(q + 1) * 512],
                              abs_scr))
            b.bank_read_done("dve", q)
            b.bank_read_done("act", q)
        return toks

    for idx in range(16):
        mod_slab_dma(idx)
        mod_slab_mms(idx)
    t_bounce_pe = repl_from_banks(sa_repA, False, 0)
    for idx in range(16, 32):
        mod_slab_dma(idx)
        mod_slab_mms(idx)
    t_bounce_pe = repl_from_banks(ca1_repA, True, t_bounce_pe)
    cast_qkv_t = [None] * 6

    def ln_group_math(st, g):
        c0, c1 = g * 4, g * 4 + 4
        sums, sqs = st[:, c0:c1], st[:, 16 + c0:16 + c1]
        m, r = st[:, 32 + c0:32 + c1], st[:, 48 + c0:48 + c1]
        sa_, sb_, sc_, sd_, se_ = (lnscr[:, 0:4], lnscr[:, 4:8], lnscr[:, 8:12],
                                   lnscr[:, 12:16], lnscr[:, 16:20])
        b.op("dve", lambda e: e.tensor_scalar(m, sums, 1.0 / D, None, OP.mult), "dve")
        b.op("dve", lambda e: e.tensor_mul(sa_, m, m), "dve")
        b.op("dve", lambda e: e.tensor_scalar(sb_, sqs, 1.0 / D, None, OP.mult), "dve")
        b.op("dve", lambda e: e.tensor_sub(sc_, sb_, sa_), "dve")
        b.op("dve", lambda e: e.tensor_scalar(sd_, sc_, 1e-5, None, OP.add), "dve")
        b.wait("act", "dve", b.cnt["dve"])
        b.op("act", lambda e: e.activation(lnsc2[:, 0:4], sd_, AF.Sqrt), "act")
        b.wait("dve", "act", b.cnt["act"])
        b.op("dve", lambda e: e.reciprocal(se_, lnsc2[:, 0:4]), "dve")
        b.op("dve", lambda e: e.tensor_mul(sa_, se_, se_), "dve")
        b.op("dve", lambda e: e.tensor_mul(sb_, sa_, sd_), "dve")
        b.op("dve", lambda e: e.tensor_scalar(sc_, sb_, -0.5, 1.5, OP.mult, OP.add), "dve")
        b.op("dve", lambda e: e.tensor_mul(r, se_, sc_), "dve")
        nmr = lnscr[:, 20:24]
        b.op("dve", lambda e: e.tensor_mul(sa_, m, r), "dve")
        return b.op("dve", lambda e: e.tensor_scalar(nmr, sa_, -1.0, None, OP.mult), "dve")

    def tm_tile_to_fm(src_tm, dst_v, tile_i, dep_dve_val, banks=(4, 5)):
        last = None
        for q in range(4):
            bk = banks[q % 2]
            b.bank_acquire(bk)
            b.wait("pe", "dve", dep_dve_val)
            for s4 in range(4):
                dc = q * 4 + s4
                b.op("pe", lambda e, o=bank4b(bk)[:, s4, :],
                     src=src_tm[:, dc * 128:(dc + 1) * 128], s=(s4 == 0),
                     p=(s4 == 3):
                     e.matmul(o, src, ident, start=s, stop=p,
                              is_transpose=True), "pe")
            b.bank_written(bk)
            b.bank_read("dve", bk)
            dst = dst_v[:, q * 4:(q + 1) * 4, tile_i * 128:(tile_i + 1) * 128]
            last = b.op("dve", lambda e, d=dst, s=bank4b(bk):
                        e.tensor_copy(d, s), "dve")
            b.bank_read_done("dve", bk)
        return last

    x_apply_t = [0] * NT
    for g in range(4):
        for i in range(g * 4, g * 4 + 4):
            slot = i % 4
            xt = xbuf[:, slot * 2048:(slot + 1) * 2048]
            if i >= 4:
                b.wait("act", "dve", x_apply_t[i - 4])
            t_x = b.dma("sc", xt, x_hbm[i * 128:(i + 1) * 128, :])
            b.wait_tok("dve", t_x)
            b.op("dve", lambda e, x=xt, o=stats[:, i:i + 1]:
                 e.reduce_sum(o, x, axis=AX), "dve")
            b.wait_tok("act", t_x)
            b.op("act", lambda e, x=xt, o=stats[:, 16 + i:17 + i]:
                 e.activation(sq_dummy, x, AF.Square, accum_out=o), "act")
        b.wait("dve", "act", b.cnt["act"])
        ln_group_math(stats, g)
        for i in range(g * 4, g * 4 + 4):
            slot = i % 4
            xt = xbuf[:, slot * 2048:(slot + 1) * 2048]
            b.op("dve", lambda e, x=xt, r=stats[:, 48 + i:49 + i],
                 nm=lnscr[:, 20 + i % 4:21 + i % 4]:
                 e.tensor_scalar(ln_u, x, r, nm, OP.mult, OP.add), "dve")
            b.op("dve", lambda e, x=xt: e.tensor_mul(x, ln_u, ca1_repA), "dve")
            t_app = b.op("dve", lambda e, x=xt: e.tensor_add(h_tm, x, sa_repA), "dve")
            x_apply_t[i] = t_app
            tm_tile_to_fm(h_tm, R1v, i, t_app)
            for s2 in range(2):
                mod_slab_dma(32 + 2 * i + s2)
                mod_slab_mms(32 + 2 * i + s2)
            if i == 7:
                toks_ga = gagb_post(0, t_bounce_pe)
                b.wait_toks("sp", toks_ga)
                b.dma("sp", ga_cols, vec_scr[0:1, :].rearrange(
                    "a (k p) -> (a p) k", p=128), slow_ok=True)
    t_h_done_pe = b.cnt["pe"]
    t_h_done_dve = b.cnt["dve"]

    t_bounce_pe = repl_from_banks(sb_rep, False, t_bounce_pe,
                                  dma_guard=toks_ga[3])
    for idx in range(64, 80):
        mod_slab_dma(idx)
        mod_slab_mms(idx)
        if idx == 71:
            for ch in (0, 2):
                cast_qkv_t[ch] = b.dma(
                    "gp", wqkv_bf[:, ch * 1024:(ch + 1) * 1024],
                    wqkv_h[:, ch * 1024:(ch + 1) * 1024])
    t_bounce_pe = repl_from_banks(cb1_rep, True, t_bounce_pe)
    for idx in range(80, 96):
        mod_slab_dma(idx)
        mod_slab_mms(idx)
        if idx == 87:
            cast_qkv_t[4] = b.dma("gp", wqkv_bf[:, 4 * 1024:5 * 1024],
                                  wqkv_h[:, 4 * 1024:5 * 1024])
    toks_gb = gagb_post(1, t_bounce_pe)
    b.wait_toks("sp", toks_gb)
    t_gbcols = b.dma("sp", gb_cols,
                     vec_scr[1:2, :].rearrange("a (k p) -> (a p) k", p=128),
                     slow_ok=True)
    b.wait_toks("dve", toks_gb)
    for ch in (1, 3, 5):
        cast_qkv_t[ch] = b.dma("gp", wqkv_bf[:, ch * 1024:(ch + 1) * 1024],
                               wqkv_h[:, ch * 1024:(ch + 1) * 1024])
    cast_wout_t = [b.dma("gp", wout_bf[:, ch * 1024:(ch + 1) * 1024],
                         wout_h[:, ch * 1024:(ch + 1) * 1024]) for ch in range(2)]

    cast_t = {}
    for name, src, dst in (("wg", wg_h, wg_bf), ("wu", wu_h, wu_bf)):
        for ch in range(16):
            cast_t[(name, ch)] = b.dma("gp", dst[ch * 512:(ch + 1) * 512, :],
                                       src[ch * 512:(ch + 1) * 512, :])

    wdf_out_t = [0] * 256
    wdf_ts_t = [0] * 256
    wgT_v = wgT_h.rearrange("(ic p dc) j -> ic p dc j", ic=NB, p=128)
    wuT_v = wuT_h.rearrange("(ic p dc) j -> ic p dc j", ic=NB, p=128)
    wdT_v = wdT_h.rearrange("(g c p s) j -> g c p s j", g=4, c=NB, p=128)

    def wdf_step(k):
        dc, part = k // 16, k % 16
        pp = k % 2
        if k == 0:
            b.wait_tok("dve", t_gbcols)
        if k >= 2:
            b.wait("sp", "dve", wdf_ts_t[k - 2])
        t_in = b.dma("sp", wdf_in[pp],
                     wd_h[dc * 128:(dc + 1) * 128, part * 512:(part + 1) * 512])
        b.wait_tok("dve", t_in)
        if k >= 2:
            b.wait_tok("dve", wdf_out_t[k - 2])
        wdf_ts_t[k] = b.op("dve", lambda e, o=wdf_out[pp], i=wdf_in[pp],
                           sc=gb_cols[:, dc:dc + 1]:
                           e.tensor_scalar(o, i, sc, None, OP.mult), "dve")
        b.wait("sp", "dve", wdf_ts_t[k])
        wdf_out_t[k] = b.dma("sp", wd_bf[dc * 128:(dc + 1) * 128,
                                         part * 512:(part + 1) * 512], wdf_out[pp])

    def emit_preps():
        last_scat = 0
        b.wait("sp", "dve", wdf_ts_t[255])
        for name, src_bf, dstv in (("wg", wg_bf, wgT_v), ("wu", wu_bf, wuT_v)):
            for dcq in range(NDC):
                for qu in range(4):
                    need_ch = ((qu + 1) * 2048 - 1) // 512
                    b.wait_toks("sp", [cast_t[(name, ch)]
                                       for ch in range(need_ch + 1)])
                    b.wait_tok("sp", last_scat)
                    t_tp = b.dma("sp", prep_buf,
                                 src_bf[qu * 2048:(qu + 1) * 2048,
                                        dcq * 128:(dcq + 1) * 128],
                                 transpose=True)
                    b.wait_tok("sp", t_tp)
                    dst = dstv[qu * 16:(qu + 1) * 16, :, dcq, :].transpose([1, 0, 2])
                    src_ap = prep_buf.rearrange("p (ic j) -> p ic j", ic=16)
                    last_scat = b.dma("sp", dst, src_ap)
                    prep_gu_toks.append(last_scat)
            prep_t[name] = last_scat
        b.wait_toks("sp", wdf_out_t)
        for cq in range(NB):
            b.wait_tok("sp", last_scat)
            t_tp = b.dma("sp", prep_buf, wd_bf[:, cq * 128:(cq + 1) * 128],
                         transpose=True)
            b.wait_tok("sp", t_tp)
            dst = wdT_v[:, cq, :, :, :].transpose([1, 0, 2, 3])
            src_ap = prep_buf.rearrange("p (g s j) -> p g s j", g=4, s=4)
            last_scat = b.dma("sp", dst, src_ap)
            prep_wd_toks.append(last_scat)
        prep_t["wd"] = last_scat

    prep_t = {}
    prep_gu_toks = []
    prep_wd_toks = []

    def emit_head(h):
        grp = h // 2
        if h % 2 == 0:
            cols0 = 4096 + grp * 256
            src = wqkv_bf[:, cols0:cols0 + 256].rearrange("(dc p) j -> p dc j", p=128)
            b.wait_tok("act", cast_qkv_t[4 + (cols0 - 4096) // 1024])
            b.wait("act", "pe", b.cnt["pe"] if grp > 0 else t_h_done_pe)
            t_wv = b.dma("sc", wv_t.rearrange("p (dc j) -> p dc j", dc=16), src)
            wv_v = wv_t.rearrange("p (dc j) -> p dc j", dc=16)
            for tt in range(16):
                bk = 2 + tt % 2
                b.bank_acquire(bk)
                b.wait_tok("pe", t_wv)
                b.wait("pe", "dve", t_h_done_dve)
                for dc in range(NDC):
                    b.op("pe", lambda e, o=bank(bk)[:, 0:256],
                         l=R1v[:, dc, tt * 128:(tt + 1) * 128],
                         r=wv_v[:, dc, :], s=(dc == 0), p=(dc == 15):
                         e.matmul(o, l, r, start=s, stop=p), "pe")
                b.bank_written(bk)
                b.bank_read("act", bk)
                b.op("act", lambda e, d=v2r[:, tt, :, 0:128],
                     s=bank(bk)[:, 0:256].rearrange("p (a c) -> p a c", a=2):
                     e.activation(d, s, AF.Copy), "act")
                b.bank_read_done("act", bk)

        for wt, dst_sb, col0 in ((wq_t, q_sb, h * 128), (wk_t, k_sb, 2048 + h * 128)):
            src = wqkv_bf[:, col0:col0 + 128].rearrange("(dc p) j -> p dc j", p=128)
            b.wait_tok("act", cast_qkv_t[col0 // 1024])
            b.wait("act", "pe", b.cnt["pe"] if h > 0 else t_h_done_pe)
            t_w = b.dma("sc", wt.rearrange("p (dc j) -> p dc j", dc=16), src)
            wv_ = wt.rearrange("p (dc j) -> p dc j", dc=16)
            for n in range(4):
                bk = 2 + n % 2
                b.bank_acquire(bk)
                b.wait_tok("pe", t_w)
                b.wait("pe", "dve", t_h_done_dve)
                for dc in range(NDC):
                    b.op("pe", lambda e, o=bank(bk), l=wv_[:, dc, :],
                         r=R1v[:, dc, n * 512:(n + 1) * 512],
                         s=(dc == 0), p=(dc == 15):
                         e.matmul(o, l, r, start=s, stop=p), "pe")
                b.bank_written(bk)
                b.bank_read("act", bk)
                b.op("act", lambda e, d=dst_sb[:, n * 512:(n + 1) * 512],
                     s=bank(bk): e.activation(d, s, AF.Copy), "act")
                b.bank_read_done("act", bk)
        t_qk_evac = b.cnt["act"]

        for sq in range(4):
            pv_banks = (0, 1, 6, 7)
            for st in range(4):
                b.bank_acquire(pv_banks[st])
            for half in range(2):
                pv_before = b.cnt["pe"]
                exp_t = {}
                for tp in range(4):
                    bk0 = 2 + (2 * tp) % 4
                    for s2 in range(2):
                        tt = half * 8 + tp * 2 + s2
                        bk = bk0 + s2
                        b.bank_acquire(bk)
                        b.wait("pe", "act", t_qk_evac)
                        b.op("pe", lambda e, o=bank(bk),
                             l=k_sb[:, tt * 128:(tt + 1) * 128],
                             r=q_sb[:, sq * 512:(sq + 1) * 512]:
                             e.matmul(o, l, r, start=True, stop=True), "pe")
                        b.bank_written(bk)
                    b.bank_read("act", bk0)
                    b.bank_read("act", bk0 + 1)
                    if half == 1:
                        b.wait("act", "pe", pv_before)
                    exp_t[tp] = b.op("act", lambda e,
                                     d=PT[:, tp * 1024:(tp + 1) * 1024],
                                     s=PS[:, 512 * bk0:512 * (bk0 + 2)]:
                                     e.activation(d, s, AF.Exp,
                                                  bias=cexp[:, 0:1],
                                                  scale=cexp[:, 1:2]), "act")
                    b.bank_read_done("act", bk0)
                    b.bank_read_done("act", bk0 + 1)
                for tti in range(8):
                    tt = half * 8 + tti
                    b.wait("pe", "act", exp_t[tti // 2])
                    for st in range(4):
                        bk = pv_banks[st]
                        b.op("pe", lambda e, o=bank(bk)[:, 0:129],
                             l=PT[:, tti * 512 + st * 128:tti * 512 + (st + 1) * 128],
                             r=v2r[:, tt, h % 2, :],
                             s=(tt == 0), p=(tt == 15):
                             e.matmul(o, l, r, start=s, stop=p), "pe")
                        if tt == 15:
                            b.bank_written(bk)
            for st in range(4):
                bk = pv_banks[st]
                b.bank_read("dve", bk)
                b.op("dve", lambda e, o=inv4[:, st:st + 1],
                     s=bank(bk)[:, 128:129]: e.reciprocal(o, s), "dve")
                b.op("dve", lambda e, d=R2v[:, sq * 4 + st, h * 128:(h + 1) * 128],
                     s=bank(bk)[:, 0:128], sc=inv4[:, st:st + 1]:
                     e.tensor_scalar(d, s, sc, None, OP.mult), "dve")
                b.bank_read_done("dve", bk)

    if phases >= 2:
        wdf_k = 0
        for h in range(H):
            emit_head(h)
            if h >= 2:
                while wdf_k < min(256, (h - 1) * 20):
                    wdf_step(wdf_k)
                    wdf_k += 1
        while wdf_k < 256:
            wdf_step(wdf_k)
            wdf_k += 1
        t_attn_dve = b.cnt["dve"]
        emit_preps()

    if phases >= 3:
        for i in range(NT):
            tm_tile_to_fm(R2v[:, i, :], R1v, i, t_attn_dve, banks=(4, 5))
        t_comb_dve = b.cnt["dve"]

        wo_parity_t = [0, 0]
        t_tr_prev = 0
        for dc in range(16):
            pp = dc % 2
            wo_buf = wo_sb[:, pp * 2048:(pp + 1) * 2048]
            src = wout_bf[:, dc * 128:(dc + 1) * 128].rearrange(
                "(dcin p) j -> p dcin j", p=128)
            b.wait_tok("act", cast_wout_t[dc // 8])
            b.wait("act", "pe", wo_parity_t[pp] if dc >= 2 else b.cnt["pe"])
            t_wo = b.dma("sc", wo_buf.rearrange("p (dcin j) -> p dcin j", dcin=16), src)
            wo_v = wo_buf.rearrange("p (dcin j) -> p dcin j", dcin=16)
            for n in range(4):
                bk = 2 + n % 2
                b.bank_acquire(bk)
                b.wait_tok("pe", t_wo)
                b.wait("pe", "dve", t_comb_dve)
                for dcin in range(16):
                    b.op("pe", lambda e, o=bank(bk), l=wo_v[:, dcin, :],
                         r=R1v[:, dcin, n * 512:(n + 1) * 512],
                         s=(dcin == 0), p=(dcin == 15):
                         e.matmul(o, l, r, start=s, stop=p), "pe")
                b.bank_written(bk)
                wo_parity_t[pp] = b.cnt["pe"]
                b.bank_read("act", bk)
                b.wait("act", "pe", t_tr_prev)
                t_ev = b.op("act", lambda e, s=bank(bk), sc=ga_cols[:, dc:dc + 1]:
                            e.activation(dout_tile, s, AF.Copy, scale=sc), "act")
                b.bank_read_done("act", bk)
                tb = 4 + n % 2
                b.bank_acquire(tb)
                b.wait("pe", "act", t_ev)
                for s4 in range(4):
                    b.op("pe", lambda e, o=bank4b(tb)[:, s4, :],
                         src=dout_tile[:, s4 * 128:(s4 + 1) * 128], s=(s4 == 0),
                         p=(s4 == 3):
                         e.matmul(o, src, ident, start=s, stop=p,
                              is_transpose=True), "pe")
                b.bank_written(tb)
                t_tr_prev = b.cnt["pe"]
                b.bank_read("dve", tb)
                dst = R2v[:, n * 4:(n + 1) * 4, dc * 128:(dc + 1) * 128]
                b.op("dve", lambda e, d=dst, s=bank4b(tb): e.tensor_copy(d, s), "dve")
                b.bank_read_done("dve", tb)

        adds_t = [0] * NT
        for i in range(NT):
            pp = i % 2
            if i >= 2:
                b.wait("act", "dve", adds_t[i - 2])
            t_x = b.dma("sc", xre[pp], x_hbm[i * 128:(i + 1) * 128, :])
            b.wait_tok("dve", t_x)
            x1t = R2v[:, i, :]
            b.op("dve", lambda e, d=x1_scr, x1=x1t, s=xre[pp]:
                 e.tensor_add(d, x1, s), "dve")
            adds_t[i] = b.op("dve", lambda e, d=x1t:
                             e.tensor_copy(d, x1_scr), "dve")
            b.op("dve", lambda e, x=x1t, o=stats2[:, i:i + 1]:
                 e.reduce_sum(o, x, axis=AX), "dve")
            b.wait("act", "dve", adds_t[i])
            b.op("act", lambda e, x=x1t, o=stats2[:, 16 + i:17 + i]:
                 e.activation(sq_d2, x, AF.Square, accum_out=o), "act")
        b.wait("dve", "act", b.cnt["act"])
        for g in range(4):
            ln_group_math(stats2, g)
        b.op("dve", lambda e: e.tensor_mul(lnscr[:, 0:16], stats2[:, 32:48],
                                           stats2[:, 48:64]), "dve")
        b.op("dve", lambda e: e.tensor_scalar(nmr2, lnscr[:, 0:16], -1.0,
                                              None, OP.mult), "dve")

    if phases >= 4:
        b.wait("sp", "act", b.cnt["act"])
        b.wait("sp", "pe", b.cnt["pe"])
        b.wait("sp", "dve", b.cnt["dve"])
        t_r1 = b.dma("sp", r1_f32, r1_h)
        b.wait_tok("dve", t_r1)
        r1tm_hi, r1tm_lo = q_sb, k_sb
        b.op("dve", lambda e: e.tensor_copy(r1tm_hi, r1_f32), "dve")
        b.op("dve", lambda e: e.tensor_sub(r1tm_lo, r1_f32, r1tm_hi), "dve")
        t_r1tm = b.cnt["dve"]
        for src_tm, dst_fm in ((r1tm_hi, r1T_hi), (r1tm_lo, r1T_lo)):
            dfm = dst_fm.rearrange("p (c j) -> p c j", c=16)
            for q in range(4):
                bk = 2 + q % 2
                b.bank_acquire(bk)
                b.wait("pe", "dve", t_r1tm)
                for s4 in range(4):
                    dc = q * 4 + s4
                    b.op("pe", lambda e, o=bank4b(bk)[:, s4, :],
                         src=src_tm[:, dc * 128:(dc + 1) * 128], s=(s4 == 0),
                         p=(s4 == 3):
                         e.matmul(o, src, ident, start=s, stop=p,
                              is_transpose=True), "pe")
                b.bank_written(bk)
                b.bank_read("dve", bk)
                b.op("dve", lambda e, d=dfm[:, q * 4:(q + 1) * 4, :],
                     s=bank4b(bk): e.tensor_copy(d, s), "dve")
                b.bank_read_done("dve", bk)
        t_r2 = b.dma("sp", r2_sb, r2_h)
        bk = 2
        b.bank_acquire(bk)
        b.wait_tok("pe", t_r2)
        b.op("pe", lambda e, o=bank(bk)[:, 0:64]: e.transpose(o, r2_sb, id64f), "pe")
        b.bank_written(bk)
        b.bank_read("dve", bk)
        b.op("dve", lambda e, s=bank(bk)[:, 0:64]: e.tensor_copy(r2T, s), "dve")
        b.bank_read_done("dve", bk)

        b.wait_toks("dve", prep_gu_toks)
        b.wait_toks("dve", prep_wd_toks)
        b.wait_toks("act", prep_gu_toks)
        b.wait_toks("act", prep_wd_toks)
        h2v_hi = h2_hi.rearrange("p (c j) -> p c j", c=16)
        h2v_lo = h2_lo.rearrange("p (c j) -> p c j", c=16)
        r1v_hi = r1T_hi.rearrange("p (c j) -> p c j", c=16)
        r1v_lo = r1T_lo.rearrange("p (c j) -> p c j", c=16)

        for s_ in range(NSTRIP):
            for k in range(4):
                i = s_ * 4 + k
                x1t = R2v[:, i, :]
                for hf in range(2):
                    xs = x1t[:, hf * 1024:(hf + 1) * 1024]
                    b.op("dve", lambda e, x=xs, r=stats2[:, 48 + i:49 + i],
                         nm=nmr2[:, i:i + 1]:
                         e.tensor_scalar(f32buf, x, r, nm, OP.mult, OP.add), "dve")
                    b.op("dve", lambda e, c=cb1_rep[:, hf * 1024:(hf + 1) * 1024]:
                         e.tensor_mul(f32buf_b, f32buf, c), "dve")
                    b.op("dve", lambda e, sbr=sb_rep[:, hf * 1024:(hf + 1) * 1024]:
                         e.tensor_add(f32buf, f32buf_b, sbr), "dve")
                    b.op("dve", lambda e: e.tensor_copy(hh, f32buf), "dve")
                    t_half = b.op("dve", lambda e: e.tensor_sub(hl, f32buf, hh), "dve")
                    for part, src in ((0, hh), (1, hl)):
                        dstv = h2v_hi if part == 0 else h2v_lo
                        for q in range(2):
                            bk = 2 + q % 2
                            b.bank_acquire(bk)
                            b.wait("pe", "dve", t_half)
                            for s4 in range(4):
                                b.op("pe", lambda e, o=bank4b(bk)[:, s4, :],
                                     src=src[:, (q * 4 + s4) * 128:
                                             (q * 4 + s4 + 1) * 128],
                                     s=(s4 == 0), p=(s4 == 3):
                                     e.matmul(o, src, ident, start=s, stop=p,
                                              is_transpose=True), "pe")
                            b.bank_written(bk)
                            b.bank_read("dve", bk)
                            dst = dstv[:, hf * 8 + q * 4:hf * 8 + (q + 1) * 4,
                                       k * 128:(k + 1) * 128]
                            b.op("dve", lambda e, d=dst, s=bank4b(bk):
                                 e.tensor_copy(d, s), "dve")
                            b.bank_read_done("dve", bk)
            t_h2 = b.cnt["dve"]

            bk = 0
            b.bank_acquire(bk)
            b.wait("pe", "dve", t_h2)
            combos = [(r1v_hi, h2v_hi), (r1v_lo, h2v_hi), (r1v_hi, h2v_lo)]
            nmm = 0
            for (wv_, hv_) in combos:
                for dc in range(16):
                    nmm += 1
                    b.op("pe", lambda e, o=bank(bk), l=wv_[:, dc, :],
                         r=hv_[:, dc, :], s=(nmm == 1), p=(nmm == 48):
                         e.matmul(o, l, r, start=s, stop=p), "pe")
            b.bank_written(bk)
            b.bank_read("act", bk)
            t_rr = b.op("act", lambda e, s=bank(bk): e.activation(rrbuf, s, AF.Silu), "act")
            b.bank_read_done("act", bk)
            for tt in range(4):
                bk = 1
                b.bank_acquire(bk)
                b.wait("pe", "act", t_rr)
                b.op("pe", lambda e, o=bank(bk)[:, 0:64],
                     l=rrbuf[:, tt * 128:(tt + 1) * 128],
                     r=r2T:
                     e.matmul(o, l, r, start=True, stop=True), "pe")
                b.bank_written(bk)
                b.bank_read("dve", bk)
                b.op("dve", lambda e, d=logits[:, tt * 64:(tt + 1) * 64],
                     s=bank(bk)[:, 0:64]: e.tensor_copy(d, s), "dve")
                b.bank_read_done("dve", bk)
                lg = logits[:, tt * 64:(tt + 1) * 64]
                lz = logits_z[:, tt * 64:(tt + 1) * 64]
                lz2 = logits_z2[:, tt * 64:(tt + 1) * 64]
                b.op("dve", lambda e, l=lg: e.max(out=m8, in_=l), "dve")
                b.op("dve", lambda e, l=lg, z=lz: e.match_replace(
                    out=z, in_to_replace=m8, in_values=l, imm_value=-1e30), "dve")
                b.op("dve", lambda e, z=lz: e.max(out=m8, in_=z), "dve")
                b.op("dve", lambda e, z=lz, z2=lz2: e.match_replace(
                    out=z2, in_to_replace=m8, in_values=z, imm_value=-1e30), "dve")
                t_msk = b.op("dve", lambda e, l=lg, z2=lz2,
                             mm=mask_tm[:, tt * 64:(tt + 1) * 64]:
                             e.tensor_tensor(mm, l, z2, OP.not_equal), "dve")
                bkm = 2
                b.bank_acquire(bkm)
                b.wait("pe", "dve", t_msk)
                b.op("pe", lambda e, src=mask_tm[:, tt * 64:(tt + 1) * 64],
                     o=bankb(bkm)[0:64, 0:128]: e.transpose(o, src, ident), "pe")
                b.bank_written(bkm)
                b.bank_read("dve", bkm)
                b.op("dve", lambda e, d=mask_fm[:, tt * 128:(tt + 1) * 128],
                     s=bankb(bkm)[0:64, 0:128]: e.tensor_copy(d, s), "dve")
                b.bank_read_done("dve", bkm)
            t_mask = b.cnt["dve"]

            wring_slots = [wring[:, j * 2048:(j + 1) * 2048] for j in range(4)]
            slot_mm_t = [0, 0, 0, 0]
            mrow_t = [0, 0, 0]
            t_si_rd = 0
            b.wait_toks("sp", prep_gu_toks)

            def mask_extract(ic):
                par = ic % 2
                mb_ = 2 if par == 0 else 5
                mslot = ic % 3
                b.bank_acquire(mb_)
                b.wait("pe", "dve", t_mask)
                b.wait("pe", "dve", mrow_t[mslot])
                b.op("pe", lambda e, o=bank(mb_)[0:1, 0:512],
                     l=id64b[:, ic:ic + 1], r=mask_fm:
                     e.matmul(o, l, r, start=True, stop=True), "pe")
                b.bank_written(mb_)
                b.bank_read("dve", mb_)
                t_mr = b.op("dve", lambda e,
                            d=mrow[0:1, mslot * 512:(mslot + 1) * 512],
                            s=bank(mb_)[0:1, 0:512]: e.tensor_copy(d, s), "dve")
                b.bank_read_done("dve", mb_)
                mrow_t[mslot] = t_mr

            mask_extract(0)
            for ic in range(NB):
                par = ic % 2
                gb_, ub_, mb_ = (0, 1, 2) if par == 0 else (3, 4, 5)
                gslot, uslot = wring_slots[2 * par], wring_slots[2 * par + 1]
                b.wait("sp", "dve", t_mask)
                if ic >= 2:
                    b.wait("sp", "pe", slot_mm_t[2 * par])
                t_wg = b.dma("sp", gslot.rearrange("p (dc j) -> p dc j", dc=16),
                             wgT_v[ic])
                t_wu = b.dma("sp", uslot.rearrange("p (dc j) -> p dc j", dc=16),
                             wuT_v[ic])
                gsv = gslot.rearrange("p (dc j) -> p dc j", dc=16)
                usv = uslot.rearrange("p (dc j) -> p dc j", dc=16)
                mslot = ic % 3
                b.bank_acquire(mb_)
                b.wait("pe", "dve", mrow_t[mslot])
                b.op("pe", lambda e, o=bank(mb_), l=ones_bf,
                     r=mrow[0:1, mslot * 512:(mslot + 1) * 512]:
                     e.matmul(o, l, r, start=True, stop=True), "pe")
                b.bank_written(mb_)
                if ic + 1 < NB:
                    mask_extract(ic + 1)
                b.bank_acquire(gb_)
                b.wait_tok("pe", t_wg)
                for dc in range(16):
                    b.op("pe", lambda e, o=bank(gb_), l=gsv[:, dc, :],
                         r=h2v_hi[:, dc, :], s=(dc == 0), p=(dc == 15):
                         e.matmul(o, l, r, start=s, stop=p), "pe")
                b.bank_written(gb_)
                b.bank_acquire(ub_)
                b.wait_tok("pe", t_wu)
                for dc in range(16):
                    b.op("pe", lambda e, o=bank(ub_), l=usv[:, dc, :],
                         r=h2v_hi[:, dc, :], s=(dc == 0), p=(dc == 15):
                         e.matmul(o, l, r, start=s, stop=p), "pe")
                b.bank_written(ub_)
                slot_mm_t[2 * par] = b.cnt["pe"]
                slot_mm_t[2 * par + 1] = b.cnt["pe"]
                b.bank_read("act", gb_)
                b.wait("act", "dve", t_si_rd)
                t_si = b.op("act", lambda e, s=bank(gb_):
                            e.activation(silu_sb, s, AF.Silu), "act")
                b.bank_read_done("act", gb_)
                b.bank_read("dve", ub_)
                b.wait("dve", "act", t_si)
                b.op("dve", lambda e, s=bank(ub_):
                     e.tensor_mul(dout_tile, silu_sb, s), "dve")
                t_si_rd = b.cnt["dve"]
                b.bank_read_done("dve", ub_)
                b.bank_read("dve", mb_)
                b.op("dve", lambda e, d=hidv[:, ic, :], s=bank(mb_):
                     e.tensor_mul(d, dout_tile, s), "dve")
                b.bank_read_done("dve", mb_)
            t_hidden = b.cnt["dve"]

            dhalf = [dring[:, 0:4096], dring[:, 4096:8192]]
            dhalf_mm_t = [0, 0]
            t_tr_prev2 = 0
            b.wait_toks("sp", prep_wd_toks)
            for g in range(4):
                banks_g = (0, 1, 2, 3) if g % 2 == 0 else (4, 5, 6, 7)
                for sub in range(4):
                    b.bank_acquire(banks_g[sub])
                for blk8 in range(8):
                    hb = blk8 % 2
                    b.wait("sp", "pe", dhalf_mm_t[hb])
                    if g == 0 and blk8 < 2:
                        b.wait("sp", "dve", t_hidden)
                    src_ap = wdT_v[g, blk8 * 8:(blk8 + 1) * 8].transpose([1, 0, 2, 3])
                    dst_ap = dhalf[hb].rearrange("p (c s j) -> p c s j", c=8, s=4)
                    t_wd = b.dma("sp", dst_ap, src_ap)
                    dsv = dst_ap
                    b.wait_tok("pe", t_wd)
                    b.wait("pe", "dve", t_hidden)
                    for ci in range(8):
                        cq = blk8 * 8 + ci
                        for sub in range(4):
                            b.op("pe", lambda e, o=bank(banks_g[sub]),
                                 l=dsv[:, ci, sub, :],
                                 r=hidv[:, cq, :], s=(cq == 0), p=(cq == NB - 1):
                                 e.matmul(o, l, r, start=s, stop=p), "pe")
                    dhalf_mm_t[hb] = b.cnt["pe"]
                for sub in range(4):
                    b.bank_written(banks_g[sub])
                for sub in range(4):
                    dc = 4 * g + sub
                    bkd = banks_g[sub]
                    tb = 4 if g % 2 == 0 else 0
                    b.bank_read("act", bkd)
                    b.wait("act", "pe", t_tr_prev2)
                    t_ev = b.op("act", lambda e, s=bank(bkd):
                                e.activation(dout_tile, s, AF.Copy), "act")
                    b.bank_read_done("act", bkd)
                    b.bank_acquire(tb)
                    b.wait("pe", "act", t_ev)
                    for s4 in range(4):
                        b.op("pe", lambda e, o=bank4b(tb)[:, s4, :],
                             src=dout_tile[:, s4 * 128:(s4 + 1) * 128],
                             s=(s4 == 0), p=(s4 == 3):
                             e.matmul(o, src, ident, start=s, stop=p,
                                      is_transpose=True), "pe")
                    b.bank_written(tb)
                    t_tr_prev2 = b.cnt["pe"]
                    b.bank_read("dve", tb)
                    dst = out_stage.rearrange("p (t j) -> p t j", t=4)[
                        :, :, sub * 128:(sub + 1) * 128]
                    x1s = R2v[:, s_ * 4:(s_ + 1) * 4, dc * 128:(dc + 1) * 128]
                    b.op("dve", lambda e, d=dst, s=bank4b(tb), x1=x1s:
                         e.tensor_add(d, s, x1), "dve")
                    b.bank_read_done("dve", tb)
                b.wait("act", "dve", b.cnt["dve"])
                store_toks = []
                for tt in range(4):
                    row0 = s_ * 512 + tt * 128
                    store_toks.append(
                        b.dma("sc", y_hbm[row0:row0 + 128, g * 512:(g + 1) * 512],
                              out_stage[:, tt * 512:(tt + 1) * 512]))
                for tk in store_toks:
                    b.wait_tok("dve", tk)

    tap_src = {"h_fm": R1, "R1": R1, "R2": R2, "A": A, "stats": stats,
               "stats2": stats2, "logits": logits, "mask_tm": mask_tm,
               "q_sb": q_sb, "k_sb": k_sb, "v2": v2_sb, "PT": PT}
    for tname, shape, dt in taps:
        b.wait_all("act")
        src = tap_src[tname]
        b.dma("sc", tap_aps[tname], src[0:shape[0], 0:shape[1]])

    b.wait_all("gp")
    b.wait_all("sp")

    import contextlib
    with contextlib.ExitStack() as ctx:
        for s in SEMS:
            b.sems[s] = ctx.enter_context(nc.semaphore(s))
        with nc.Block() as block:
            @block.tensor
            def _(e):
                for f in b.plan["pe"]:
                    f(e)

            @block.scalar
            def _(e):
                for f in b.plan["act"]:
                    f(e)

            @block.vector
            def _(e):
                for f in b.plan["dve"]:
                    f(e)

            @block.sync
            def _(e):
                for f in b.plan["sp"]:
                    f(e)

            @block.gpsimd
            def _(e):
                for f in b.plan["gp"]:
                    f(e)
    return nc


_CACHE = {}


def _get_nc():
    if "nc" not in _CACHE:
        _CACHE["nc"] = build_nc(phases=4)
    return _CACHE["nc"]


def kernel(**inputs):
    nc = _get_nc()
    x = np.ascontiguousarray(np.asarray(inputs["L0_attn_x"], dtype=np.float32))
    shared = {
        "w_qkv": np.ascontiguousarray(inputs["L0_attn_w_qkv"], dtype=np.float32),
        "w_out": np.ascontiguousarray(inputs["L0_attn_w_out"], dtype=np.float32),
        "gate_w": np.ascontiguousarray(inputs["L0_ffn_gate_w"], dtype=np.float32),
        "up_w": np.ascontiguousarray(inputs["L0_ffn_up_w"], dtype=np.float32),
        "down_w": np.ascontiguousarray(inputs["L0_ffn_down_w"], dtype=np.float32),
        "router_w1": np.ascontiguousarray(inputs["L0_ffn_router_w1"], dtype=np.float32),
        "router_w2": np.ascontiguousarray(inputs["L0_ffn_router_w2"], dtype=np.float32),
        "wmod": np.ascontiguousarray(inputs["L0_wmod"], dtype=np.float32),
    }
    c_all = np.ascontiguousarray(np.asarray(inputs["L0_c"], dtype=np.float32))
    per_core = []
    for core in range(8):
        m = dict(shared)
        m["x"] = np.ascontiguousarray(x[core])
        m["c"] = np.ascontiguousarray(c_all[core:core + 1])
        per_core.append(m)
    res = run_bass_kernel_spmd(nc, per_core, core_ids=list(range(8)))
    return np.stack([res.results[i]["y"] for i in range(8)], axis=0)


# version 13
# speedup vs baseline: 1.1288x; 1.1288x over previous
import numpy as np

import concourse.bass as bass
import concourse.mybir as mybir
from concourse.bass_utils import run_bass_kernel_spmd

F32 = mybir.dt.float32
F32R = mybir.dt.float32r
BF16 = mybir.dt.bfloat16
AF = mybir.ActivationFunctionType
OP = mybir.AluOpType
AX = mybir.AxisListType.X

S, D, NT, NDC, H, I, NB = 2048, 2048, 16, 16, 16, 8192, 64
STRIP, NSTRIP = 512, 4
EXP_BIAS = -12.0
QK_SCALE = 1.0 / 128.0 ** 0.5

ENGINES = ("pe", "act", "dve", "sp", "gp")
NCH = {"sp": 7, "gp": 5, "sc": 4}
QSEMS = tuple(f"q_{q}{i}" for q in NCH for i in range(NCH[q]))
SEMS = ("pe", "act", "dve", "gp_e") + QSEMS
OWN = {"pe": "pe", "act": "act", "dve": "dve"}


OWN_ALL = {"pe": "pe", "act": "act", "dve": "dve", "gp": "gp_e"}


class B:
    def __init__(self, nc, debug_sync=False):
        self.nc = nc
        self.debug_sync = debug_sync
        self.plan = {e: [] for e in ENGINES}
        self.cnt = {s: 0 for s in SEMS}
        self.sems = {}
        self.bank_ready = [0] * 8
        self.bank_free = [[] for _ in range(8)]
        self.qrr = {"sp": 0, "gp": 0, "sc": 0}

    def emit(self, eng, fn):
        self.plan[eng].append(fn)

    def wait(self, eng, sem, val):
        if val is None or val <= 0:
            return
        if OWN.get(eng) == sem:
            assert val <= self.cnt[sem], "self-wait on future value would hang"
            return
        sems = self.sems

        def f(e, sem=sem, val=val):
            e.wait_ge(sems[sem], val)
        self.emit(eng, f)

    AUTO_SELF = ("dve", "act", "gp")

    def op(self, eng, fn, sem=None, n=1):
        if (self.debug_sync and eng in OWN_ALL) or eng in self.AUTO_SELF:
            own = OWN_ALL[eng]
            val = self.cnt[own]
            if val > 0:
                sems = self.sems
                self.emit(eng, lambda e, own=own, val=val:
                          e.wait_ge(sems[own], val))
        if sem is None:
            self.emit(eng, lambda e, fn=fn: fn(e))
            return None
        self.cnt[sem] += n
        tgt = self.cnt[sem]
        sems = self.sems

        def f(e, fn=fn, sem=sem, n=n):
            fn(e).then_inc(sems[sem], n)
        self.emit(eng, f)
        return tgt

    def dma(self, queue, out_ap, in_ap, transpose=False, slow_ok=False):
        eng = {"sp": "sp", "gp": "gp", "sc": "act"}[queue]
        ch = self.qrr[queue] % NCH[queue]
        self.qrr[queue] += 1
        sem = f"q_{queue}{ch}"
        self.wait(eng, sem, self.cnt[sem])
        if transpose:
            fn = lambda e, o=out_ap, i=in_ap: e.dma_start(o, i, transpose=True)
        elif slow_ok:
            fn = lambda e, o=out_ap, i=in_ap: e.dma_start(
                o, i, allow_slow_non_contiguous=True)
        else:
            fn = lambda e, o=out_ap, i=in_ap: e.dma_start(o, i)
        val = self.op(eng, fn, sem, 16)
        return (sem, val)

    def wait_tok(self, eng, tok):
        if tok is None or tok == 0:
            return
        if isinstance(tok, tuple):
            self.wait(eng, tok[0], tok[1])
        else:
            raise TypeError(tok)

    def wait_toks(self, eng, toks):
        best = {}
        for t in toks:
            if t is None or t == 0:
                continue
            best[t[0]] = max(best.get(t[0], 0), t[1])
        for sem, val in sorted(best.items()):
            self.wait(eng, sem, val)

    def bank_acquire(self, i):
        for (sem, val) in self.bank_free[i]:
            self.wait("pe", sem, val)
        self.bank_free[i] = []

    def bank_written(self, i):
        self.bank_ready[i] = self.cnt["pe"]

    def bank_read(self, eng, i):
        self.wait(eng, "pe", self.bank_ready[i])

    def bank_read_done(self, eng, i):
        self.bank_free[i].append((OWN[eng], self.cnt[OWN[eng]]))

    def wait_all(self, eng):
        for s in SEMS:
            self.wait(eng, s, self.cnt[s])


def build_nc(phases=4, taps=(), debug_sync=False):
    nc = bass.Bass("TRN2", target_bir_lowering=False,
                   detect_race_conditions=debug_sync)
    b = B(nc, debug_sync=debug_sync)

    x_hbm = nc.dram_tensor("x", [S, D], F32, kind="ExternalInput").ap()
    wqkv_h = nc.dram_tensor("w_qkv", [D, 3 * D], F32, kind="ExternalInput").ap()
    wout_h = nc.dram_tensor("w_out", [D, D], F32, kind="ExternalInput").ap()
    wg_h = nc.dram_tensor("gate_w", [I, D], F32, kind="ExternalInput").ap()
    wu_h = nc.dram_tensor("up_w", [I, D], F32, kind="ExternalInput").ap()
    wd_h = nc.dram_tensor("down_w", [D, I], F32, kind="ExternalInput").ap()
    r1_h = nc.dram_tensor("router_w1", [128, D], F32, kind="ExternalInput").ap()
    r2_h = nc.dram_tensor("router_w2", [NB, 128], F32, kind="ExternalInput").ap()
    c_h = nc.dram_tensor("c", [1, D], F32, kind="ExternalInput").ap()
    wmod_h = nc.dram_tensor("wmod", [D, 6 * D], F32, kind="ExternalInput").ap()
    y_hbm = nc.dram_tensor("y", [S, D], F32, kind="ExternalOutput").ap()

    wqkv_bf = nc.dram_tensor("wqkv_bf", [D, 3 * D], BF16, kind="Internal").ap()
    wout_bf = nc.dram_tensor("wout_bf", [D, D], BF16, kind="Internal").ap()
    wg_bf = nc.dram_tensor("wg_bf", [I, D], BF16, kind="Internal").ap()
    wu_bf = nc.dram_tensor("wu_bf", [I, D], BF16, kind="Internal").ap()
    wd_bf = nc.dram_tensor("wd_bf", [D, I], BF16, kind="Internal").ap()
    wgT_h = nc.dram_tensor("wgT", [NB * 128 * NDC, 128], BF16, kind="Internal").ap()
    wuT_h = nc.dram_tensor("wuT", [NB * 128 * NDC, 128], BF16, kind="Internal").ap()
    wdT_h = nc.dram_tensor("wdT", [4 * NB * 128 * 4, 128], BF16, kind="Internal").ap()
    vec_scr = nc.dram_tensor("vec_scr", [2, D], F32, kind="Internal").ap()

    tap_aps = {}
    for tname, shape, dt in taps:
        tap_aps[tname] = nc.dram_tensor("tap_" + tname, list(shape), dt,
                                        kind="ExternalOutput").ap()

    alloc = nc.alloc_sbuf_tensor
    ones_f = alloc("ones_f", [1, 128], F32).ap()
    ones_bf = alloc("ones_bf", [1, 128], BF16).ap()
    ident = alloc("ident", [128, 128], BF16).ap()
    id64b = alloc("id64b", [64, 64], BF16).ap()
    id64f = alloc("id64f", [64, 64], F32).ap()
    c_sb = alloc("c_sb", [128, 16], F32).ap()
    silu_c = alloc("silu_c", [128, 16], F32).ap()
    silu_cb = alloc("silu_cb", [128, 16], BF16).ap()
    ga_cols = alloc("ga_cols", [128, 16], F32).ap()
    gb_cols = alloc("gb_cols", [128, 16], F32).ap()
    stats = alloc("stats", [128, 64], F32).ap()
    stats2 = alloc("stats2", [128, 64], F32).ap()
    lnscr = alloc("lnscr", [128, 32], F32).ap()
    lnsc2 = alloc("lnsc2", [128, 16], F32).ap()
    nmr2 = alloc("nmr2", [128, 16], F32).ap()
    inv4 = alloc("inv4", [128, 4], F32).ap()
    cb1_rep = alloc("cb1_rep", [128, D], BF16).ap()
    sb_rep = alloc("sb_rep", [128, D], BF16).ap()
    r1T_hi = alloc("r1T_hi", [128, D], BF16).ap()
    r1T_lo = alloc("r1T_lo", [128, D], BF16).ap()
    r2T = alloc("r2T", [128, NB], F32).ap()
    r2_sb = alloc("r2_sb", [64, 128], F32).ap()
    logits = alloc("logits", [128, 4 * NB], F32).ap()
    logits_z = alloc("logits_z", [128, 4 * NB], F32).ap()
    logits_z2 = alloc("logits_z2", [128, 4 * NB], F32).ap()
    m8 = alloc("m8", [128, 8], F32).ap()
    mask_tm = alloc("mask_tm", [128, 4 * NB], BF16).ap()
    mask_fm = alloc("mask_fm", [64, STRIP], BF16).ap()
    dout_tile = alloc("dout_tile", [128, 512], BF16).ap()
    silu_sb = alloc("silu_sb", [128, 512], BF16).ap()
    cexp = alloc("cexp", [128, 2], F32).ap()

    R1 = alloc("R1", [128, 32768], BF16).ap()
    R2 = alloc("R2", [128, 32768], BF16).ap()
    A = alloc("A", [128, 24576], BF16).ap()

    PS = nc.alloc_psum_tensor("PS", [128, 4096], F32).ap()

    def bank(i):
        return PS[:, 512 * i:512 * (i + 1)]

    def bank4(i):
        return bank(i).rearrange("p (a c) -> p a c", a=4)

    def bankb(i):
        return bank(i).bitcast(BF16)

    def bank4b(i):
        return bankb(i)[:, 0:512].rearrange("p (a c) -> p a c", a=4)

    def fv(base, el, n, parts=128):
        assert el % 2 == 0 and el + 2 * n <= base.shape[1]
        return base[0:parts, el:el + 2 * n].bitcast(F32)

    xbuf = fv(R2, 16384, 8192)
    sq_dummy = R2[:, 12288:14336]
    h_tm = R2[:, 14336:16384]
    ln_u = fv(R2, 8192, 2048)
    R2v = R2.rearrange("p (c t) -> p c t", c=16)

    PT = A[:, 0:4096]
    xre = (fv(A, 0, 2048), fv(A, 4096, 2048))
    q_sb = A[:, 4096:6144]
    k_sb = A[:, 6144:8192]
    v2_sb = A[:, 8192:12320]
    v2r = v2_sb.rearrange("p (t h j) -> p t h j", t=16, h=2)
    wq_t = A[:, 12320:14368]
    wk_t = A[:, 14368:16416]
    wv_t = A[:, 16416:20512]
    prep_buf = A[:, 20512:22560]
    wdf_in = (fv(A, 20512, 512), fv(A, 21536, 512))
    wdf_out = (A[:, 22560:23072], A[:, 23072:23584])
    r1_f32 = fv(A, 8192, 2048)
    sq_d2 = A[:, 8192:10240]
    x1_scr = A[:, 10240:12288]
    h2_hi = A[:, 0:8192]
    h2_lo = A[:, 8192:16384]
    wring = A[:, 8192:16384]
    dring = A[:, 8192:16384]
    f32buf = fv(A, 16384, 1024)
    hh = A[:, 18432:19456]
    hl = A[:, 19456:20480]
    rrbuf = fv(A, 20512, 512)
    f32buf_b = fv(A, 21536, 1024)
    wo_sb = A[:, 12320:16416]
    out_stage = fv(A, 0, 2048)
    mrow = A[0:1, 18432:19968]

    R1v = R1.rearrange("p (c t) -> p c t", c=16)
    hidv = R1.rearrange("p (c j) -> p c j", c=NB)

    sa_repA = fv(A, 0, 2048)
    ca1_repA = fv(A, 4096, 2048)

    b.op("dve", lambda e: e.memset(ones_f, 1.0), "dve")
    b.op("dve", lambda e: e.memset(ones_bf, 1.0), "dve")
    b.op("dve", lambda e: e.memset(cexp[:, 0:1], EXP_BIAS), "dve")
    t_consts = b.op("dve", lambda e: e.memset(cexp[:, 1:2], QK_SCALE), "dve")
    t_ms = b.op("gp", lambda e: e.memset(ident, 0.0), "gp_e")
    b.wait("gp", "gp_e", t_ms)
    b.op("gp", lambda e: e.affine_select(out=ident, in_=ident,
         compare_op=OP.not_equal, fill=1.0, base=0, pattern=[[-1, 128]],
         channel_multiplier=1), "gp_e")
    for idt in (id64b, id64f):
        t_ms = b.op("gp", lambda e, t=idt: e.memset(t, 0.0), "gp_e")
        b.wait("gp", "gp_e", t_ms)
        b.op("gp", lambda e, t=idt: e.affine_select(out=t, in_=t,
             compare_op=OP.not_equal, fill=1.0, base=0, pattern=[[-1, 64]],
             channel_multiplier=1), "gp_e")
    b.op("dve", lambda e: e.memset(v2r[:, :, :, 128:129], 1.0), "dve")

    t = b.dma("gp", c_sb, c_h.rearrange("a (k p) -> (a p) k", p=128), slow_ok=True)
    b.wait_tok("act", t)
    b.wait("act", "dve", t_consts)
    b.op("act", lambda e: e.activation(silu_c, c_sb, AF.Silu), "act")
    t_siluc = b.op("act", lambda e: e.activation(silu_cb, c_sb, AF.Silu), "act")

    slab_tok = {}
    slab_mm = {}

    def ring_slot(idx):
        return R2[:, (idx % 2) * 2048:(idx % 2) * 2048 + 2048]

    def mod_slab_dma(idx):
        if idx >= 2:
            b.wait("gp", "pe", slab_mm[idx - 2])
        g, kc = divmod(idx, 16)
        slab_tok[idx] = b.dma("gp", ring_slot(idx),
                              wmod_h[kc * 128:(kc + 1) * 128,
                                     g * 2048:(g + 1) * 2048])

    def mod_slab_mms(idx):
        g, kc = divmod(idx, 16)
        slot = ring_slot(idx)
        b.wait_tok("pe", slab_tok[idx])
        if idx == 0:
            b.wait("pe", "act", t_siluc)
        for q in range(4):
            if kc == 0:
                b.bank_acquire(q)
            b.op("pe", lambda e, o=bank(q)[0:1, :], l=silu_cb[:, kc:kc + 1],
                 r=slot[:, q * 512:(q + 1) * 512], s=(kc == 0), p=(kc == 15):
                 e.matmul(o, l, r, start=s, stop=p), "pe")
            if kc == 15:
                b.bank_written(q)
        slab_mm[idx] = b.cnt["pe"]

    bounce = [fv(R2, 4096 + 1024 * j, 512)[0:1, :] for j in range(4)]

    def repl_from_banks(dst_ap, plus1, pe_guard, dma_guard=None):
        if pe_guard:
            b.wait("dve", "pe", pe_guard)
        if dma_guard is not None:
            b.wait_tok("dve", dma_guard)
        cp_t = [0] * 4
        for q in range(4):
            b.bank_read("dve", q)
            cp_t[q] = b.op("dve", lambda e, d=bounce[q], s=bank(q)[0:1, :]:
                           e.tensor_copy(d, s), "dve")
            b.bank_read_done("dve", q)
        for q in range(4):
            bk = 6 + q % 2
            b.bank_acquire(bk)
            b.wait("pe", "dve", cp_t[q])
            b.op("pe", lambda e, o=bank(bk), l=ones_f, r=bounce[q]:
                 e.matmul(o, l, r, start=True, stop=True), "pe")
            b.bank_written(bk)
            b.bank_read("dve", bk)
            if plus1:
                b.op("dve", lambda e, d=dst_ap[:, q * 512:(q + 1) * 512],
                     s=bank(bk): e.tensor_scalar(d, s, 1.0, None, OP.add), "dve")
            else:
                b.op("dve", lambda e, d=dst_ap[:, q * 512:(q + 1) * 512],
                     s=bank(bk): e.tensor_copy(d, s), "dve")
            b.bank_read_done("dve", bk)
        return b.cnt["pe"]

    def gagb_post(row, pe_guard):
        sgn_scr, sgn2, abs_scr, abs2 = bounce
        if pe_guard:
            b.wait("dve", "pe", pe_guard)
            b.wait("act", "pe", pe_guard)
        toks = []
        for q in range(4):
            b.bank_read("dve", q)
            b.bank_read("act", q)
            src = bank(q)[0:1, :]
            b.op("dve", lambda e, s=src: e.tensor_scalar(
                sgn_scr, s, 0.0, 2.0, OP.is_ge, OP.mult), "dve")
            b.op("dve", lambda e: e.tensor_scalar(
                sgn2, sgn_scr, -1.0, None, OP.add), "dve")
            b.wait("act", "dve", b.cnt["dve"])
            if q > 0:
                b.wait_tok("act", toks[q - 1])
            b.op("act", lambda e, s=src: e.activation(abs_scr, s, AF.Abs), "act")
            b.wait("dve", "act", b.cnt["act"])
            b.op("dve", lambda e: e.tensor_scalar(abs2, abs_scr, 0.25, None,
                                                  OP.max), "dve")
            b.op("dve", lambda e: e.tensor_mul(abs_scr, sgn2, abs2), "dve")
            b.wait("act", "dve", b.cnt["dve"])
            toks.append(b.dma("sc", vec_scr[row:row + 1, q * 512:(q + 1) * 512],
                              abs_scr))
            b.bank_read_done("dve", q)
            b.bank_read_done("act", q)
        return toks

    for idx in range(16):
        mod_slab_dma(idx)
        mod_slab_mms(idx)
    t_bounce_pe = repl_from_banks(sa_repA, False, 0)
    for idx in range(16, 32):
        mod_slab_dma(idx)
        mod_slab_mms(idx)
    t_bounce_pe = repl_from_banks(ca1_repA, True, t_bounce_pe)
    cast_qkv_t = [None] * 12

    def ln_group_math(st, g):
        c0, c1 = g * 4, g * 4 + 4
        sums, sqs = st[:, c0:c1], st[:, 16 + c0:16 + c1]
        m, r = st[:, 32 + c0:32 + c1], st[:, 48 + c0:48 + c1]
        sa_, sb_, sc_, sd_, se_ = (lnscr[:, 0:4], lnscr[:, 4:8], lnscr[:, 8:12],
                                   lnscr[:, 12:16], lnscr[:, 16:20])
        b.op("dve", lambda e: e.tensor_scalar(m, sums, 1.0 / D, None, OP.mult), "dve")
        b.op("dve", lambda e: e.tensor_mul(sa_, m, m), "dve")
        b.op("dve", lambda e: e.tensor_scalar(sb_, sqs, 1.0 / D, None, OP.mult), "dve")
        b.op("dve", lambda e: e.tensor_sub(sc_, sb_, sa_), "dve")
        b.op("dve", lambda e: e.tensor_scalar(sd_, sc_, 1e-5, None, OP.add), "dve")
        b.wait("act", "dve", b.cnt["dve"])
        b.op("act", lambda e: e.activation(lnsc2[:, 0:4], sd_, AF.Sqrt), "act")
        b.wait("dve", "act", b.cnt["act"])
        b.op("dve", lambda e: e.reciprocal(se_, lnsc2[:, 0:4]), "dve")
        b.op("dve", lambda e: e.tensor_mul(sa_, se_, se_), "dve")
        b.op("dve", lambda e: e.tensor_mul(sb_, sa_, sd_), "dve")
        b.op("dve", lambda e: e.tensor_scalar(sc_, sb_, -0.5, 1.5, OP.mult, OP.add), "dve")
        b.op("dve", lambda e: e.tensor_mul(r, se_, sc_), "dve")
        nmr = lnscr[:, 20:24]
        b.op("dve", lambda e: e.tensor_mul(sa_, m, r), "dve")
        return b.op("dve", lambda e: e.tensor_scalar(nmr, sa_, -1.0, None, OP.mult), "dve")

    def tm_tile_to_fm(src_tm, dst_v, tile_i, dep_dve_val, banks=(4, 5)):
        last = None
        for q in range(4):
            bk = banks[q % 2]
            b.bank_acquire(bk)
            b.wait("pe", "dve", dep_dve_val)
            for s4 in range(4):
                dc = q * 4 + s4
                b.op("pe", lambda e, o=bank4b(bk)[:, s4, :],
                     src=src_tm[:, dc * 128:(dc + 1) * 128], s=(s4 == 0),
                     p=(s4 == 3):
                     e.matmul(o, src, ident, start=s, stop=p,
                              is_transpose=True), "pe")
            b.bank_written(bk)
            b.bank_read("dve", bk)
            dst = dst_v[:, q * 4:(q + 1) * 4, tile_i * 128:(tile_i + 1) * 128]
            last = b.op("dve", lambda e, d=dst, s=bank4b(bk):
                        e.tensor_copy(d, s), "dve")
            b.bank_read_done("dve", bk)
        return last

    x_apply_t = [0] * NT
    for g in range(4):
        for i in range(g * 4, g * 4 + 4):
            slot = i % 4
            xt = xbuf[:, slot * 2048:(slot + 1) * 2048]
            if i >= 4:
                b.wait("act", "dve", x_apply_t[i - 4])
            t_x = b.dma("sc", xt, x_hbm[i * 128:(i + 1) * 128, :])
            b.wait_tok("dve", t_x)
            b.op("dve", lambda e, x=xt, o=stats[:, i:i + 1]:
                 e.reduce_sum(o, x, axis=AX), "dve")
            b.wait_tok("act", t_x)
            b.op("act", lambda e, x=xt, o=stats[:, 16 + i:17 + i]:
                 e.activation(sq_dummy, x, AF.Square, accum_out=o), "act")
        b.wait("dve", "act", b.cnt["act"])
        ln_group_math(stats, g)
        for i in range(g * 4, g * 4 + 4):
            slot = i % 4
            xt = xbuf[:, slot * 2048:(slot + 1) * 2048]
            b.op("dve", lambda e, x=xt, r=stats[:, 48 + i:49 + i],
                 nm=lnscr[:, 20 + i % 4:21 + i % 4]:
                 e.tensor_scalar(ln_u, x, r, nm, OP.mult, OP.add), "dve")
            b.op("dve", lambda e, x=xt: e.tensor_mul(x, ln_u, ca1_repA), "dve")
            t_app = b.op("dve", lambda e, x=xt: e.tensor_add(h_tm, x, sa_repA), "dve")
            x_apply_t[i] = t_app
            tm_tile_to_fm(h_tm, R1v, i, t_app)
            for s2 in range(2):
                mod_slab_dma(32 + 2 * i + s2)
                mod_slab_mms(32 + 2 * i + s2)
            if i == 7:
                toks_ga = gagb_post(0, t_bounce_pe)
                b.wait_toks("sp", toks_ga)
                t_gacols = b.dma("sp", ga_cols, vec_scr[0:1, :].rearrange(
                    "a (k p) -> (a p) k", p=128), slow_ok=True)
    t_h_done_pe = b.cnt["pe"]
    t_h_done_dve = b.cnt["dve"]

    def cast_half(hh):
        cast_qkv_t[hh] = b.dma("gp", wqkv_bf[:, hh * 512:(hh + 1) * 512],
                               wqkv_h[:, hh * 512:(hh + 1) * 512])
    t_bounce_pe = repl_from_banks(sb_rep, False, t_bounce_pe,
                                  dma_guard=toks_ga[3])
    for idx in range(64, 80):
        mod_slab_dma(idx)
        mod_slab_mms(idx)
        if idx == 71:
            cast_half(0)
            cast_half(4)
    t_bounce_pe = repl_from_banks(cb1_rep, True, t_bounce_pe)
    for idx in range(80, 96):
        mod_slab_dma(idx)
        mod_slab_mms(idx)
        if idx == 87:
            cast_half(8)
    toks_gb = gagb_post(1, t_bounce_pe)
    b.wait_toks("sp", toks_gb)
    t_gbcols = b.dma("sp", gb_cols,
                     vec_scr[1:2, :].rearrange("a (k p) -> (a p) k", p=128),
                     slow_ok=True)
    b.wait_toks("dve", toks_gb)
    for ch_half in (1, 5, 9, 2, 6, 10, 3, 7, 11):
        cast_half(ch_half)
    cast_wout_t = [b.dma("gp", wout_bf[:, ch * 1024:(ch + 1) * 1024],
                         wout_h[:, ch * 1024:(ch + 1) * 1024]) for ch in range(2)]

    cast_t = {}
    for ch in range(16):
        cast_t[("wg", ch)] = b.dma("gp", wg_bf[ch * 512:(ch + 1) * 512, :],
                                   wg_h[ch * 512:(ch + 1) * 512, :])
    for ch in range(8):
        cast_t[("wd", ch)] = b.dma("gp", wd_bf[ch * 256:(ch + 1) * 256, :],
                                   wd_h[ch * 256:(ch + 1) * 256, :])
    for ch in range(16):
        cast_t[("wu", ch)] = b.dma("gp", wu_bf[ch * 512:(ch + 1) * 512, :],
                                   wu_h[ch * 512:(ch + 1) * 512, :])

    wgT_v = wgT_h.rearrange("(ic p dc) j -> ic p dc j", ic=NB, p=128)
    wuT_v = wuT_h.rearrange("(ic p dc) j -> ic p dc j", ic=NB, p=128)
    wdT_v = wdT_h.rearrange("(g c p s) j -> g c p s j", g=4, c=NB, p=128)
    prep_t = {}
    prep_gu_toks = []
    prep_wd_toks = []
    prep_last = [0]

    def gu_prep_step(s):
        name = "wg" if s < 64 else "wu"
        dcq, qu = (s % 64) // 4, s % 4
        src_bf = wg_bf if s < 64 else wu_bf
        dstv = wgT_v if s < 64 else wuT_v
        need_ch = ((qu + 1) * 2048 - 1) // 512
        b.wait_toks("sp", [cast_t[(name, ch)] for ch in range(need_ch + 1)])
        b.wait_tok("sp", prep_last[0])
        t_tp = b.dma("sp", prep_buf,
                     src_bf[qu * 2048:(qu + 1) * 2048,
                            dcq * 128:(dcq + 1) * 128],
                     transpose=True)
        b.wait_tok("sp", t_tp)
        dst = dstv[qu * 16:(qu + 1) * 16, :, dcq, :].transpose([1, 0, 2])
        src_ap = prep_buf.rearrange("p (ic j) -> p ic j", ic=16)
        prep_last[0] = b.dma("sp", dst, src_ap)
        prep_gu_toks.append(prep_last[0])
        prep_t[name] = prep_last[0]

    def wd_prep_step(cq):
        if cq == 0:
            b.wait_toks("sp", [cast_t[("wd", ch)] for ch in range(8)])
        b.wait_tok("sp", prep_last[0])
        t_tp = b.dma("sp", prep_buf, wd_bf[:, cq * 128:(cq + 1) * 128],
                     transpose=True)
        b.wait_tok("sp", t_tp)
        dst = wdT_v[:, cq, :, :, :].transpose([1, 0, 2, 3])
        src_ap = prep_buf.rearrange("p (g s j) -> p g s j", g=4, s=4)
        prep_last[0] = b.dma("sp", dst, src_ap)
        prep_wd_toks.append(prep_last[0])
        prep_t["wd"] = prep_last[0]

    def emit_head(h):
        grp = h // 2
        if h % 2 == 0:
            cols0 = 4096 + grp * 256
            src = wqkv_bf[:, cols0:cols0 + 256].rearrange("(dc p) j -> p dc j", p=128)
            b.wait_tok("act", cast_qkv_t[cols0 // 512])
            b.wait("act", "pe", b.cnt["pe"] if grp > 0 else t_h_done_pe)
            t_wv = b.dma("sc", wv_t.rearrange("p (dc j) -> p dc j", dc=16), src)
            wv_v = wv_t.rearrange("p (dc j) -> p dc j", dc=16)
            for tt in range(16):
                bk = 2 + tt % 2
                b.bank_acquire(bk)
                b.wait_tok("pe", t_wv)
                b.wait("pe", "dve", t_h_done_dve)
                for dc in range(NDC):
                    b.op("pe", lambda e, o=bank(bk)[:, 0:256],
                         l=R1v[:, dc, tt * 128:(tt + 1) * 128],
                         r=wv_v[:, dc, :], s=(dc == 0), p=(dc == 15):
                         e.matmul(o, l, r, start=s, stop=p), "pe")
                b.bank_written(bk)
                b.bank_read("act", bk)
                b.op("act", lambda e, d=v2r[:, tt, :, 0:128],
                     s=bank(bk)[:, 0:256].rearrange("p (a c) -> p a c", a=2):
                     e.activation(d, s, AF.Copy), "act")
                b.bank_read_done("act", bk)

        for wt, dst_sb, col0 in ((wq_t, q_sb, h * 128), (wk_t, k_sb, 2048 + h * 128)):
            src = wqkv_bf[:, col0:col0 + 128].rearrange("(dc p) j -> p dc j", p=128)
            b.wait_tok("act", cast_qkv_t[col0 // 512])
            b.wait("act", "pe", b.cnt["pe"] if h > 0 else t_h_done_pe)
            t_w = b.dma("sc", wt.rearrange("p (dc j) -> p dc j", dc=16), src)
            wv_ = wt.rearrange("p (dc j) -> p dc j", dc=16)
            for n in range(4):
                bk = 2 + n % 2
                b.bank_acquire(bk)
                b.wait_tok("pe", t_w)
                b.wait("pe", "dve", t_h_done_dve)
                for dc in range(NDC):
                    b.op("pe", lambda e, o=bank(bk), l=wv_[:, dc, :],
                         r=R1v[:, dc, n * 512:(n + 1) * 512],
                         s=(dc == 0), p=(dc == 15):
                         e.matmul(o, l, r, start=s, stop=p), "pe")
                b.bank_written(bk)
                b.bank_read("act", bk)
                b.op("act", lambda e, d=dst_sb[:, n * 512:(n + 1) * 512],
                     s=bank(bk): e.activation(d, s, AF.Copy), "act")
                b.bank_read_done("act", bk)
        t_qk_evac = b.cnt["act"]

        for sq in range(4):
            pv_banks = (0, 1, 6, 7)
            for st in range(4):
                b.bank_acquire(pv_banks[st])
            for half in range(2):
                pv_before = b.cnt["pe"]
                exp_t = {}
                for tp in range(4):
                    bk0 = 2 + (2 * tp) % 4
                    for s2 in range(2):
                        tt = half * 8 + tp * 2 + s2
                        bk = bk0 + s2
                        b.bank_acquire(bk)
                        b.wait("pe", "act", t_qk_evac)
                        b.op("pe", lambda e, o=bank(bk),
                             l=k_sb[:, tt * 128:(tt + 1) * 128],
                             r=q_sb[:, sq * 512:(sq + 1) * 512]:
                             e.matmul(o, l, r, start=True, stop=True), "pe")
                        b.bank_written(bk)
                    b.bank_read("act", bk0)
                    b.bank_read("act", bk0 + 1)
                    if half == 1:
                        b.wait("act", "pe", pv_before)
                    exp_t[tp] = b.op("act", lambda e,
                                     d=PT[:, tp * 1024:(tp + 1) * 1024],
                                     s=PS[:, 512 * bk0:512 * (bk0 + 2)]:
                                     e.activation(d, s, AF.Exp,
                                                  bias=cexp[:, 0:1],
                                                  scale=cexp[:, 1:2]), "act")
                    b.bank_read_done("act", bk0)
                    b.bank_read_done("act", bk0 + 1)
                for tti in range(8):
                    tt = half * 8 + tti
                    b.wait("pe", "act", exp_t[tti // 2])
                    for st in range(4):
                        bk = pv_banks[st]
                        b.op("pe", lambda e, o=bank(bk)[:, 0:129],
                             l=PT[:, tti * 512 + st * 128:tti * 512 + (st + 1) * 128],
                             r=v2r[:, tt, h % 2, :],
                             s=(tt == 0), p=(tt == 15):
                             e.matmul(o, l, r, start=s, stop=p), "pe")
                        if tt == 15:
                            b.bank_written(bk)
            for st in range(4):
                bk = pv_banks[st]
                b.bank_read("dve", bk)
                b.op("dve", lambda e, o=inv4[:, st:st + 1],
                     s=bank(bk)[:, 128:129]: e.reciprocal(o, s), "dve")
                b.op("dve", lambda e, d=R2v[:, sq * 4 + st, h * 128:(h + 1) * 128],
                     s=bank(bk)[:, 0:128], sc=inv4[:, st:st + 1]:
                     e.tensor_scalar(d, s, sc, None, OP.mult), "dve")
                b.bank_read_done("dve", bk)

    if phases >= 2:
        gu_k = 0
        wd_k = 0
        for h in range(H):
            emit_head(h)
            if h >= 2:
                while gu_k < min(128, (h - 1) * 12):
                    gu_prep_step(gu_k)
                    gu_k += 1
            if h >= 12:
                while wd_k < min(NB, (h - 11) * 16):
                    wd_prep_step(wd_k)
                    wd_k += 1
        while gu_k < 128:
            gu_prep_step(gu_k)
            gu_k += 1
        while wd_k < NB:
            wd_prep_step(wd_k)
            wd_k += 1
        t_attn_dve = b.cnt["dve"]

    if phases >= 3:
        for i in range(NT):
            tm_tile_to_fm(R2v[:, i, :], R1v, i, t_attn_dve, banks=(4, 5))
        t_comb_dve = b.cnt["dve"]

        wo_parity_t = [0, 0]
        t_tr_prev = 0
        for dc in range(16):
            pp = dc % 2
            wo_buf = wo_sb[:, pp * 2048:(pp + 1) * 2048]
            src = wout_bf[:, dc * 128:(dc + 1) * 128].rearrange(
                "(dcin p) j -> p dcin j", p=128)
            b.wait_tok("act", cast_wout_t[dc // 8])
            b.wait("act", "pe", wo_parity_t[pp] if dc >= 2 else b.cnt["pe"])
            t_wo = b.dma("sc", wo_buf.rearrange("p (dcin j) -> p dcin j", dcin=16), src)
            wo_v = wo_buf.rearrange("p (dcin j) -> p dcin j", dcin=16)
            for n in range(4):
                bk = 2 + n % 2
                b.bank_acquire(bk)
                b.wait_tok("pe", t_wo)
                b.wait("pe", "dve", t_comb_dve)
                for dcin in range(16):
                    b.op("pe", lambda e, o=bank(bk), l=wo_v[:, dcin, :],
                         r=R1v[:, dcin, n * 512:(n + 1) * 512],
                         s=(dcin == 0), p=(dcin == 15):
                         e.matmul(o, l, r, start=s, stop=p), "pe")
                b.bank_written(bk)
                wo_parity_t[pp] = b.cnt["pe"]
                b.bank_read("act", bk)
                b.wait("act", "pe", t_tr_prev)
                b.wait_tok("act", t_gacols)
                t_ev = b.op("act", lambda e, s=bank(bk), sc=ga_cols[:, dc:dc + 1]:
                            e.activation(dout_tile, s, AF.Copy, scale=sc), "act")
                b.bank_read_done("act", bk)
                tb = 4 + n % 2
                b.bank_acquire(tb)
                b.wait("pe", "act", t_ev)
                for s4 in range(4):
                    b.op("pe", lambda e, o=bank4b(tb)[:, s4, :],
                         src=dout_tile[:, s4 * 128:(s4 + 1) * 128], s=(s4 == 0),
                         p=(s4 == 3):
                         e.matmul(o, src, ident, start=s, stop=p,
                              is_transpose=True), "pe")
                b.bank_written(tb)
                t_tr_prev = b.cnt["pe"]
                b.bank_read("dve", tb)
                dst = R2v[:, n * 4:(n + 1) * 4, dc * 128:(dc + 1) * 128]
                b.op("dve", lambda e, d=dst, s=bank4b(tb): e.tensor_copy(d, s), "dve")
                b.bank_read_done("dve", tb)

        adds_t = [0] * NT
        for i in range(NT):
            pp = i % 2
            if i >= 2:
                b.wait("act", "dve", adds_t[i - 2])
            t_x = b.dma("sc", xre[pp], x_hbm[i * 128:(i + 1) * 128, :])
            b.wait_tok("dve", t_x)
            x1t = R2v[:, i, :]
            b.op("dve", lambda e, d=x1_scr, x1=x1t, s=xre[pp]:
                 e.tensor_add(d, x1, s), "dve")
            adds_t[i] = b.op("dve", lambda e, d=x1t:
                             e.tensor_copy(d, x1_scr), "dve")
            b.op("dve", lambda e, x=x1t, o=stats2[:, i:i + 1]:
                 e.reduce_sum(o, x, axis=AX), "dve")
            b.wait("act", "dve", adds_t[i])
            b.op("act", lambda e, x=x1t, o=stats2[:, 16 + i:17 + i]:
                 e.activation(sq_d2, x, AF.Square, accum_out=o), "act")
        b.wait("dve", "act", b.cnt["act"])
        for g in range(4):
            ln_group_math(stats2, g)
        b.op("dve", lambda e: e.tensor_mul(lnscr[:, 0:16], stats2[:, 32:48],
                                           stats2[:, 48:64]), "dve")
        b.op("dve", lambda e: e.tensor_scalar(nmr2, lnscr[:, 0:16], -1.0,
                                              None, OP.mult), "dve")

    if phases >= 4:
        b.wait("sp", "act", b.cnt["act"])
        b.wait("sp", "pe", b.cnt["pe"])
        b.wait("sp", "dve", b.cnt["dve"])
        t_r1 = b.dma("sp", r1_f32, r1_h)
        b.wait_tok("dve", t_r1)
        r1tm_hi, r1tm_lo = q_sb, k_sb
        b.op("dve", lambda e: e.tensor_copy(r1tm_hi, r1_f32), "dve")
        b.op("dve", lambda e: e.tensor_sub(r1tm_lo, r1_f32, r1tm_hi), "dve")
        t_r1tm = b.cnt["dve"]
        for src_tm, dst_fm in ((r1tm_hi, r1T_hi), (r1tm_lo, r1T_lo)):
            dfm = dst_fm.rearrange("p (c j) -> p c j", c=16)
            for q in range(4):
                bk = 2 + q % 2
                b.bank_acquire(bk)
                b.wait("pe", "dve", t_r1tm)
                for s4 in range(4):
                    dc = q * 4 + s4
                    b.op("pe", lambda e, o=bank4b(bk)[:, s4, :],
                         src=src_tm[:, dc * 128:(dc + 1) * 128], s=(s4 == 0),
                         p=(s4 == 3):
                         e.matmul(o, src, ident, start=s, stop=p,
                              is_transpose=True), "pe")
                b.bank_written(bk)
                b.bank_read("dve", bk)
                b.op("dve", lambda e, d=dfm[:, q * 4:(q + 1) * 4, :],
                     s=bank4b(bk): e.tensor_copy(d, s), "dve")
                b.bank_read_done("dve", bk)
        t_r2 = b.dma("sp", r2_sb, r2_h)
        bk = 2
        b.bank_acquire(bk)
        b.wait_tok("pe", t_r2)
        b.op("pe", lambda e, o=bank(bk)[:, 0:64]: e.transpose(o, r2_sb, id64f), "pe")
        b.bank_written(bk)
        b.bank_read("dve", bk)
        b.op("dve", lambda e, s=bank(bk)[:, 0:64]: e.tensor_copy(r2T, s), "dve")
        b.bank_read_done("dve", bk)

        b.wait_toks("dve", prep_gu_toks)
        b.wait_toks("dve", prep_wd_toks)
        b.wait_toks("act", prep_gu_toks)
        b.wait_toks("act", prep_wd_toks)
        h2v_hi = h2_hi.rearrange("p (c j) -> p c j", c=16)
        h2v_lo = h2_lo.rearrange("p (c j) -> p c j", c=16)
        r1v_hi = r1T_hi.rearrange("p (c j) -> p c j", c=16)
        r1v_lo = r1T_lo.rearrange("p (c j) -> p c j", c=16)

        for s_ in range(NSTRIP):
            for k in range(4):
                i = s_ * 4 + k
                x1t = R2v[:, i, :]
                for hf in range(2):
                    xs = x1t[:, hf * 1024:(hf + 1) * 1024]
                    b.op("dve", lambda e, x=xs, r=stats2[:, 48 + i:49 + i],
                         nm=nmr2[:, i:i + 1]:
                         e.tensor_scalar(f32buf, x, r, nm, OP.mult, OP.add), "dve")
                    b.op("dve", lambda e, c=cb1_rep[:, hf * 1024:(hf + 1) * 1024]:
                         e.tensor_mul(f32buf_b, f32buf, c), "dve")
                    b.op("dve", lambda e, sbr=sb_rep[:, hf * 1024:(hf + 1) * 1024]:
                         e.tensor_add(f32buf, f32buf_b, sbr), "dve")
                    b.op("dve", lambda e: e.tensor_copy(hh, f32buf), "dve")
                    t_half = b.op("dve", lambda e: e.tensor_sub(hl, f32buf, hh), "dve")
                    for part, src in ((0, hh), (1, hl)):
                        dstv = h2v_hi if part == 0 else h2v_lo
                        for q in range(2):
                            bk = 2 + q % 2
                            b.bank_acquire(bk)
                            b.wait("pe", "dve", t_half)
                            for s4 in range(4):
                                b.op("pe", lambda e, o=bank4b(bk)[:, s4, :],
                                     src=src[:, (q * 4 + s4) * 128:
                                             (q * 4 + s4 + 1) * 128],
                                     s=(s4 == 0), p=(s4 == 3):
                                     e.matmul(o, src, ident, start=s, stop=p,
                                              is_transpose=True), "pe")
                            b.bank_written(bk)
                            b.bank_read("dve", bk)
                            dst = dstv[:, hf * 8 + q * 4:hf * 8 + (q + 1) * 4,
                                       k * 128:(k + 1) * 128]
                            b.op("dve", lambda e, d=dst, s=bank4b(bk):
                                 e.tensor_copy(d, s), "dve")
                            b.bank_read_done("dve", bk)
            t_h2 = b.cnt["dve"]

            bk = 0
            b.bank_acquire(bk)
            b.wait("pe", "dve", t_h2)
            combos = [(r1v_hi, h2v_hi), (r1v_lo, h2v_hi), (r1v_hi, h2v_lo)]
            nmm = 0
            for (wv_, hv_) in combos:
                for dc in range(16):
                    nmm += 1
                    b.op("pe", lambda e, o=bank(bk), l=wv_[:, dc, :],
                         r=hv_[:, dc, :], s=(nmm == 1), p=(nmm == 48):
                         e.matmul(o, l, r, start=s, stop=p), "pe")
            b.bank_written(bk)
            b.bank_read("act", bk)
            t_rr = b.op("act", lambda e, s=bank(bk): e.activation(rrbuf, s, AF.Silu), "act")
            b.bank_read_done("act", bk)
            for tt in range(4):
                bk = 1
                b.bank_acquire(bk)
                b.wait("pe", "act", t_rr)
                b.op("pe", lambda e, o=bank(bk)[:, 0:64],
                     l=rrbuf[:, tt * 128:(tt + 1) * 128],
                     r=r2T:
                     e.matmul(o, l, r, start=True, stop=True), "pe")
                b.bank_written(bk)
                b.bank_read("dve", bk)
                b.op("dve", lambda e, d=logits[:, tt * 64:(tt + 1) * 64],
                     s=bank(bk)[:, 0:64]: e.tensor_copy(d, s), "dve")
                b.bank_read_done("dve", bk)
                lg = logits[:, tt * 64:(tt + 1) * 64]
                lz = logits_z[:, tt * 64:(tt + 1) * 64]
                lz2 = logits_z2[:, tt * 64:(tt + 1) * 64]
                b.op("dve", lambda e, l=lg: e.max(out=m8, in_=l), "dve")
                b.op("dve", lambda e, l=lg, z=lz: e.match_replace(
                    out=z, in_to_replace=m8, in_values=l, imm_value=-1e30), "dve")
                b.op("dve", lambda e, z=lz: e.max(out=m8, in_=z), "dve")
                b.op("dve", lambda e, z=lz, z2=lz2: e.match_replace(
                    out=z2, in_to_replace=m8, in_values=z, imm_value=-1e30), "dve")
                t_msk = b.op("dve", lambda e, l=lg, z2=lz2,
                             mm=mask_tm[:, tt * 64:(tt + 1) * 64]:
                             e.tensor_tensor(mm, l, z2, OP.not_equal), "dve")
                bkm = 2
                b.bank_acquire(bkm)
                b.wait("pe", "dve", t_msk)
                b.op("pe", lambda e, src=mask_tm[:, tt * 64:(tt + 1) * 64],
                     o=bankb(bkm)[0:64, 0:128]: e.transpose(o, src, ident), "pe")
                b.bank_written(bkm)
                b.bank_read("dve", bkm)
                b.op("dve", lambda e, d=mask_fm[:, tt * 128:(tt + 1) * 128],
                     s=bankb(bkm)[0:64, 0:128]: e.tensor_copy(d, s), "dve")
                b.bank_read_done("dve", bkm)
            t_mask = b.cnt["dve"]

            wring_slots = [wring[:, j * 2048:(j + 1) * 2048] for j in range(4)]
            slot_mm_t = [0, 0, 0, 0]
            mrow_t = [0, 0, 0]
            t_si_rd = 0
            b.wait_toks("sp", prep_gu_toks)

            def mask_extract(ic):
                par = ic % 2
                mb_ = 2 if par == 0 else 5
                mslot = ic % 3
                b.bank_acquire(mb_)
                b.wait("pe", "dve", t_mask)
                b.wait("pe", "dve", mrow_t[mslot])
                b.op("pe", lambda e, o=bank(mb_)[0:1, 0:512],
                     l=id64b[:, ic:ic + 1], r=mask_fm:
                     e.matmul(o, l, r, start=True, stop=True), "pe")
                b.bank_written(mb_)
                b.bank_read("dve", mb_)
                t_mr = b.op("dve", lambda e,
                            d=mrow[0:1, mslot * 512:(mslot + 1) * 512],
                            s=bank(mb_)[0:1, 0:512]: e.tensor_copy(d, s), "dve")
                b.bank_read_done("dve", mb_)
                mrow_t[mslot] = t_mr

            mask_extract(0)
            for ic in range(NB):
                par = ic % 2
                gb_, ub_, mb_ = (0, 1, 2) if par == 0 else (3, 4, 5)
                gslot, uslot = wring_slots[2 * par], wring_slots[2 * par + 1]
                b.wait("sp", "dve", t_mask)
                if ic >= 2:
                    b.wait("sp", "pe", slot_mm_t[2 * par])
                t_wg = b.dma("sp", gslot.rearrange("p (dc j) -> p dc j", dc=16),
                             wgT_v[ic])
                t_wu = b.dma("sp", uslot.rearrange("p (dc j) -> p dc j", dc=16),
                             wuT_v[ic])
                gsv = gslot.rearrange("p (dc j) -> p dc j", dc=16)
                usv = uslot.rearrange("p (dc j) -> p dc j", dc=16)
                mslot = ic % 3
                b.bank_acquire(mb_)
                b.wait("pe", "dve", mrow_t[mslot])
                b.op("pe", lambda e, o=bank(mb_), l=ones_bf,
                     r=mrow[0:1, mslot * 512:(mslot + 1) * 512]:
                     e.matmul(o, l, r, start=True, stop=True), "pe")
                b.bank_written(mb_)
                if ic + 1 < NB:
                    mask_extract(ic + 1)
                b.bank_acquire(gb_)
                b.wait_tok("pe", t_wg)
                for dc in range(16):
                    b.op("pe", lambda e, o=bank(gb_), l=gsv[:, dc, :],
                         r=h2v_hi[:, dc, :], s=(dc == 0), p=(dc == 15):
                         e.matmul(o, l, r, start=s, stop=p), "pe")
                b.bank_written(gb_)
                b.bank_acquire(ub_)
                b.wait_tok("pe", t_wu)
                for dc in range(16):
                    b.op("pe", lambda e, o=bank(ub_), l=usv[:, dc, :],
                         r=h2v_hi[:, dc, :], s=(dc == 0), p=(dc == 15):
                         e.matmul(o, l, r, start=s, stop=p), "pe")
                b.bank_written(ub_)
                slot_mm_t[2 * par] = b.cnt["pe"]
                slot_mm_t[2 * par + 1] = b.cnt["pe"]
                b.bank_read("act", gb_)
                b.wait("act", "dve", t_si_rd)
                t_si = b.op("act", lambda e, s=bank(gb_):
                            e.activation(silu_sb, s, AF.Silu), "act")
                b.bank_read_done("act", gb_)
                b.bank_read("dve", ub_)
                b.wait("dve", "act", t_si)
                b.op("dve", lambda e, s=bank(ub_):
                     e.tensor_mul(dout_tile, silu_sb, s), "dve")
                t_si_rd = b.cnt["dve"]
                b.bank_read_done("dve", ub_)
                b.bank_read("dve", mb_)
                b.op("dve", lambda e, d=hidv[:, ic, :], s=bank(mb_):
                     e.tensor_mul(d, dout_tile, s), "dve")
                b.bank_read_done("dve", mb_)
            t_hidden = b.cnt["dve"]

            dhalf = [dring[:, 0:4096], dring[:, 4096:8192]]
            dhalf_mm_t = [0, 0]
            t_tr_prev2 = 0
            b.wait_toks("sp", prep_wd_toks)
            for g in range(4):
                banks_g = (0, 1, 2, 3) if g % 2 == 0 else (4, 5, 6, 7)
                for sub in range(4):
                    b.bank_acquire(banks_g[sub])
                for blk8 in range(8):
                    hb = blk8 % 2
                    b.wait("sp", "pe", dhalf_mm_t[hb])
                    if g == 0 and blk8 < 2:
                        b.wait("sp", "dve", t_hidden)
                    src_ap = wdT_v[g, blk8 * 8:(blk8 + 1) * 8].transpose([1, 0, 2, 3])
                    dst_ap = dhalf[hb].rearrange("p (c s j) -> p c s j", c=8, s=4)
                    t_wd = b.dma("sp", dst_ap, src_ap)
                    dsv = dst_ap
                    b.wait_tok("pe", t_wd)
                    b.wait("pe", "dve", t_hidden)
                    for ci in range(8):
                        cq = blk8 * 8 + ci
                        for sub in range(4):
                            b.op("pe", lambda e, o=bank(banks_g[sub]),
                                 l=dsv[:, ci, sub, :],
                                 r=hidv[:, cq, :], s=(cq == 0), p=(cq == NB - 1):
                                 e.matmul(o, l, r, start=s, stop=p), "pe")
                    dhalf_mm_t[hb] = b.cnt["pe"]
                for sub in range(4):
                    b.bank_written(banks_g[sub])
                for sub in range(4):
                    dc = 4 * g + sub
                    bkd = banks_g[sub]
                    tb = 4 if g % 2 == 0 else 0
                    b.bank_read("act", bkd)
                    b.wait("act", "pe", t_tr_prev2)
                    if s_ == 0 and g == 0 and sub == 0:
                        b.wait_tok("act", t_gbcols)
                    t_ev = b.op("act", lambda e, s=bank(bkd),
                                sc=gb_cols[:, dc:dc + 1]:
                                e.activation(dout_tile, s, AF.Copy, scale=sc),
                                "act")
                    b.bank_read_done("act", bkd)
                    b.bank_acquire(tb)
                    b.wait("pe", "act", t_ev)
                    for s4 in range(4):
                        b.op("pe", lambda e, o=bank4b(tb)[:, s4, :],
                             src=dout_tile[:, s4 * 128:(s4 + 1) * 128],
                             s=(s4 == 0), p=(s4 == 3):
                             e.matmul(o, src, ident, start=s, stop=p,
                                      is_transpose=True), "pe")
                    b.bank_written(tb)
                    t_tr_prev2 = b.cnt["pe"]
                    b.bank_read("dve", tb)
                    dst = out_stage.rearrange("p (t j) -> p t j", t=4)[
                        :, :, sub * 128:(sub + 1) * 128]
                    x1s = R2v[:, s_ * 4:(s_ + 1) * 4, dc * 128:(dc + 1) * 128]
                    b.op("dve", lambda e, d=dst, s=bank4b(tb), x1=x1s:
                         e.tensor_add(d, s, x1), "dve")
                    b.bank_read_done("dve", tb)
                b.wait("act", "dve", b.cnt["dve"])
                store_toks = []
                for tt in range(4):
                    row0 = s_ * 512 + tt * 128
                    store_toks.append(
                        b.dma("sc", y_hbm[row0:row0 + 128, g * 512:(g + 1) * 512],
                              out_stage[:, tt * 512:(tt + 1) * 512]))
                for tk in store_toks:
                    b.wait_tok("dve", tk)

    tap_src = {"h_fm": R1, "R1": R1, "R2": R2, "A": A, "stats": stats,
               "stats2": stats2, "logits": logits, "mask_tm": mask_tm,
               "q_sb": q_sb, "k_sb": k_sb, "v2": v2_sb, "PT": PT}
    for tname, shape, dt in taps:
        b.wait_all("act")
        src = tap_src[tname]
        b.dma("sc", tap_aps[tname], src[0:shape[0], 0:shape[1]])

    b.wait_all("gp")
    b.wait_all("sp")

    import contextlib
    with contextlib.ExitStack() as ctx:
        for s in SEMS:
            b.sems[s] = ctx.enter_context(nc.semaphore(s))
        with nc.Block() as block:
            @block.tensor
            def _(e):
                for f in b.plan["pe"]:
                    f(e)

            @block.scalar
            def _(e):
                for f in b.plan["act"]:
                    f(e)

            @block.vector
            def _(e):
                for f in b.plan["dve"]:
                    f(e)

            @block.sync
            def _(e):
                for f in b.plan["sp"]:
                    f(e)

            @block.gpsimd
            def _(e):
                for f in b.plan["gp"]:
                    f(e)
    return nc


_CACHE = {}


def _get_nc():
    if "nc" not in _CACHE:
        _CACHE["nc"] = build_nc(phases=4)
    return _CACHE["nc"]


def kernel(**inputs):
    nc = _get_nc()
    x = np.ascontiguousarray(np.asarray(inputs["L0_attn_x"], dtype=np.float32))
    shared = {
        "w_qkv": np.ascontiguousarray(inputs["L0_attn_w_qkv"], dtype=np.float32),
        "w_out": np.ascontiguousarray(inputs["L0_attn_w_out"], dtype=np.float32),
        "gate_w": np.ascontiguousarray(inputs["L0_ffn_gate_w"], dtype=np.float32),
        "up_w": np.ascontiguousarray(inputs["L0_ffn_up_w"], dtype=np.float32),
        "down_w": np.ascontiguousarray(inputs["L0_ffn_down_w"], dtype=np.float32),
        "router_w1": np.ascontiguousarray(inputs["L0_ffn_router_w1"], dtype=np.float32),
        "router_w2": np.ascontiguousarray(inputs["L0_ffn_router_w2"], dtype=np.float32),
        "wmod": np.ascontiguousarray(inputs["L0_wmod"], dtype=np.float32),
    }
    c_all = np.ascontiguousarray(np.asarray(inputs["L0_c"], dtype=np.float32))
    per_core = []
    for core in range(8):
        m = dict(shared)
        m["x"] = np.ascontiguousarray(x[core])
        m["c"] = np.ascontiguousarray(c_all[core:core + 1])
        per_core.append(m)
    res = run_bass_kernel_spmd(nc, per_core, core_ids=list(range(8)))
    return np.stack([res.results[i]["y"] for i in range(8)], axis=0)


# version 14 (final)
# speedup vs baseline: 1.1607x; 1.0283x over previous
import numpy as np

import concourse.bass as bass
import concourse.mybir as mybir
from concourse.bass_utils import run_bass_kernel_spmd

F32 = mybir.dt.float32
F32R = mybir.dt.float32r
BF16 = mybir.dt.bfloat16
AF = mybir.ActivationFunctionType
OP = mybir.AluOpType
AX = mybir.AxisListType.X

S, D, NT, NDC, H, I, NB = 2048, 2048, 16, 16, 16, 8192, 64
STRIP, NSTRIP = 512, 4
EXP_BIAS = -12.0
QK_SCALE = 1.0 / 128.0 ** 0.5

ENGINES = ("pe", "act", "dve", "sp", "gp")
NCH = {"sp": 7, "gp": 5, "sc": 4}
QSEMS = tuple(f"q_{q}{i}" for q in NCH for i in range(NCH[q]))
SEMS = ("pe", "act", "dve", "gp_e") + QSEMS
OWN = {"pe": "pe", "act": "act", "dve": "dve"}


OWN_ALL = {"pe": "pe", "act": "act", "dve": "dve", "gp": "gp_e"}


class B:
    def __init__(self, nc, debug_sync=False):
        self.nc = nc
        self.debug_sync = debug_sync
        self.plan = {e: [] for e in ENGINES}
        self.cnt = {s: 0 for s in SEMS}
        self.sems = {}
        self.bank_ready = [0] * 8
        self.bank_free = [[] for _ in range(8)]
        self.qrr = {"sp": 0, "gp": 0, "sc": 0}

    def emit(self, eng, fn):
        self.plan[eng].append(fn)

    def wait(self, eng, sem, val):
        if val is None or val <= 0:
            return
        if OWN.get(eng) == sem:
            assert val <= self.cnt[sem], "self-wait on future value would hang"
            return
        sems = self.sems

        def f(e, sem=sem, val=val):
            e.wait_ge(sems[sem], val)
        self.emit(eng, f)

    AUTO_SELF = ("dve", "act", "gp")

    def op(self, eng, fn, sem=None, n=1):
        if (self.debug_sync and eng in OWN_ALL) or eng in self.AUTO_SELF:
            own = OWN_ALL[eng]
            val = self.cnt[own]
            if val > 0:
                sems = self.sems
                self.emit(eng, lambda e, own=own, val=val:
                          e.wait_ge(sems[own], val))
        if sem is None:
            self.emit(eng, lambda e, fn=fn: fn(e))
            return None
        self.cnt[sem] += n
        tgt = self.cnt[sem]
        sems = self.sems

        def f(e, fn=fn, sem=sem, n=n):
            fn(e).then_inc(sems[sem], n)
        self.emit(eng, f)
        return tgt

    def dma(self, queue, out_ap, in_ap, transpose=False, slow_ok=False):
        eng = {"sp": "sp", "gp": "gp", "sc": "act"}[queue]
        ch = self.qrr[queue] % NCH[queue]
        self.qrr[queue] += 1
        sem = f"q_{queue}{ch}"
        self.wait(eng, sem, self.cnt[sem])
        if transpose:
            fn = lambda e, o=out_ap, i=in_ap: e.dma_start(o, i, transpose=True)
        elif slow_ok:
            fn = lambda e, o=out_ap, i=in_ap: e.dma_start(
                o, i, allow_slow_non_contiguous=True)
        else:
            fn = lambda e, o=out_ap, i=in_ap: e.dma_start(o, i)
        val = self.op(eng, fn, sem, 16)
        return (sem, val)

    def wait_tok(self, eng, tok):
        if tok is None or tok == 0:
            return
        if isinstance(tok, tuple):
            self.wait(eng, tok[0], tok[1])
        else:
            raise TypeError(tok)

    def wait_toks(self, eng, toks):
        best = {}
        for t in toks:
            if t is None or t == 0:
                continue
            best[t[0]] = max(best.get(t[0], 0), t[1])
        for sem, val in sorted(best.items()):
            self.wait(eng, sem, val)

    def bank_acquire(self, i):
        for (sem, val) in self.bank_free[i]:
            self.wait("pe", sem, val)
        self.bank_free[i] = []

    def bank_written(self, i):
        self.bank_ready[i] = self.cnt["pe"]

    def bank_read(self, eng, i):
        self.wait(eng, "pe", self.bank_ready[i])

    def bank_read_done(self, eng, i):
        self.bank_free[i].append((OWN[eng], self.cnt[OWN[eng]]))

    def wait_all(self, eng):
        for s in SEMS:
            self.wait(eng, s, self.cnt[s])


def build_nc(phases=4, taps=(), debug_sync=False):
    nc = bass.Bass("TRN2", target_bir_lowering=False,
                   detect_race_conditions=debug_sync)
    b = B(nc, debug_sync=debug_sync)

    x_hbm = nc.dram_tensor("x", [S, D], F32, kind="ExternalInput").ap()
    wqkv_h = nc.dram_tensor("w_qkv", [D, 3 * D], F32, kind="ExternalInput").ap()
    wout_h = nc.dram_tensor("w_out", [D, D], F32, kind="ExternalInput").ap()
    wg_h = nc.dram_tensor("gate_w", [I, D], F32, kind="ExternalInput").ap()
    wu_h = nc.dram_tensor("up_w", [I, D], F32, kind="ExternalInput").ap()
    wd_h = nc.dram_tensor("down_w", [D, I], F32, kind="ExternalInput").ap()
    r1_h = nc.dram_tensor("router_w1", [128, D], F32, kind="ExternalInput").ap()
    r2_h = nc.dram_tensor("router_w2", [NB, 128], F32, kind="ExternalInput").ap()
    c_h = nc.dram_tensor("c", [1, D], F32, kind="ExternalInput").ap()
    wmod_h = nc.dram_tensor("wmod", [D, 6 * D], F32, kind="ExternalInput").ap()
    y_hbm = nc.dram_tensor("y", [S, D], F32, kind="ExternalOutput").ap()

    wqkv_bf = nc.dram_tensor("wqkv_bf", [D, 3 * D], BF16, kind="Internal").ap()
    wout_bf = nc.dram_tensor("wout_bf", [D, D], BF16, kind="Internal").ap()
    wg_bf = nc.dram_tensor("wg_bf", [I, D], BF16, kind="Internal").ap()
    wu_bf = nc.dram_tensor("wu_bf", [I, D], BF16, kind="Internal").ap()
    wd_bf = nc.dram_tensor("wd_bf", [D, I], BF16, kind="Internal").ap()
    wgT_h = nc.dram_tensor("wgT", [NB * 128 * NDC, 128], BF16, kind="Internal").ap()
    wuT_h = nc.dram_tensor("wuT", [NB * 128 * NDC, 128], BF16, kind="Internal").ap()
    wdT_h = nc.dram_tensor("wdT", [4 * NB * 128 * 4, 128], BF16, kind="Internal").ap()
    vec_scr = nc.dram_tensor("vec_scr", [2, D], F32, kind="Internal").ap()

    tap_aps = {}
    for tname, shape, dt in taps:
        tap_aps[tname] = nc.dram_tensor("tap_" + tname, list(shape), dt,
                                        kind="ExternalOutput").ap()

    alloc = nc.alloc_sbuf_tensor
    ones_f = alloc("ones_f", [1, 128], F32).ap()
    ones_bf = alloc("ones_bf", [1, 128], BF16).ap()
    ident = alloc("ident", [128, 128], BF16).ap()
    id64b = alloc("id64b", [64, 64], BF16).ap()
    id64f = alloc("id64f", [64, 64], F32).ap()
    c_sb = alloc("c_sb", [128, 16], F32).ap()
    silu_c = alloc("silu_c", [128, 16], F32).ap()
    silu_cb = alloc("silu_cb", [128, 16], BF16).ap()
    ga_cols = alloc("ga_cols", [128, 16], F32).ap()
    gb_cols = alloc("gb_cols", [128, 16], F32).ap()
    stats = alloc("stats", [128, 64], F32).ap()
    stats2 = alloc("stats2", [128, 64], F32).ap()
    lnscr = alloc("lnscr", [128, 32], F32).ap()
    lnsc2 = alloc("lnsc2", [128, 16], F32).ap()
    nmr2 = alloc("nmr2", [128, 16], F32).ap()
    inv4 = alloc("inv4", [128, 4], F32).ap()
    cb1_rep = alloc("cb1_rep", [128, D], BF16).ap()
    sb_rep = alloc("sb_rep", [128, D], BF16).ap()
    r1T_hi = alloc("r1T_hi", [128, D], BF16).ap()
    r1T_lo = alloc("r1T_lo", [128, D], BF16).ap()
    r2T = alloc("r2T", [128, NB], F32).ap()
    r2_sb = alloc("r2_sb", [64, 128], F32).ap()
    logits = alloc("logits", [128, 4 * NB], F32).ap()
    logits_z = alloc("logits_z", [128, 4 * NB], F32).ap()
    logits_z2 = alloc("logits_z2", [128, 4 * NB], F32).ap()
    m8 = alloc("m8", [128, 8], F32).ap()
    mask_tm = alloc("mask_tm", [128, 4 * NB], BF16).ap()
    mask_fm = alloc("mask_fm", [64, STRIP], BF16).ap()
    dout_tile = alloc("dout_tile", [128, 512], BF16).ap()
    silu_sb = alloc("silu_sb", [128, 512], BF16).ap()
    cexp = alloc("cexp", [128, 2], F32).ap()

    R1 = alloc("R1", [128, 32768], BF16).ap()
    R2 = alloc("R2", [128, 32768], BF16).ap()
    A = alloc("A", [128, 24576], BF16).ap()

    PS = nc.alloc_psum_tensor("PS", [128, 4096], F32).ap()

    def bank(i):
        return PS[:, 512 * i:512 * (i + 1)]

    def bank4(i):
        return bank(i).rearrange("p (a c) -> p a c", a=4)

    def bankb(i):
        return bank(i).bitcast(BF16)

    def bank4b(i):
        return bankb(i)[:, 0:512].rearrange("p (a c) -> p a c", a=4)

    def fv(base, el, n, parts=128):
        assert el % 2 == 0 and el + 2 * n <= base.shape[1]
        return base[0:parts, el:el + 2 * n].bitcast(F32)

    xbuf = fv(R2, 16384, 8192)
    sq_dummy = R2[:, 12288:14336]
    h_tm = R2[:, 14336:16384]
    ln_u = fv(R2, 8192, 2048)
    R2v = R2.rearrange("p (c t) -> p c t", c=16)

    PT = A[:, 0:4096]
    xre = (fv(A, 0, 2048), fv(A, 4096, 2048))
    q_sb = A[:, 4096:6144]
    k_sb = A[:, 6144:8192]
    v2_sb = A[:, 8192:12320]
    v2r = v2_sb.rearrange("p (t h j) -> p t h j", t=16, h=2)
    wq_t = A[:, 12320:14368]
    wk_t = A[:, 14368:16416]
    wv_t = A[:, 16416:20512]
    prep_buf = A[:, 20512:22560]
    wdf_in = (fv(A, 20512, 512), fv(A, 21536, 512))
    wdf_out = (A[:, 22560:23072], A[:, 23072:23584])
    r1_f32 = fv(A, 8192, 2048)
    sq_d2 = A[:, 8192:10240]
    x1_scr = A[:, 10240:12288]
    h2_hi = A[:, 0:8192]
    h2_lo = A[:, 8192:16384]
    wring = A[:, 8192:16384]
    dring = A[:, 8192:16384]
    f32buf = fv(A, 16384, 1024)
    hh = A[:, 18432:19456]
    hl = A[:, 19456:20480]
    rrbuf = fv(A, 20512, 512)
    f32buf_b = fv(A, 21536, 1024)
    wo_sb = A[:, 12320:16416]
    out_stage = fv(A, 0, 2048)
    mrow = A[0:1, 18432:19968]

    R1v = R1.rearrange("p (c t) -> p c t", c=16)
    hidv = R1.rearrange("p (c j) -> p c j", c=NB)

    sa_repA = fv(A, 0, 2048)
    ca1_repA = fv(A, 4096, 2048)

    b.op("dve", lambda e: e.memset(ones_f, 1.0), "dve")
    b.op("dve", lambda e: e.memset(ones_bf, 1.0), "dve")
    b.op("dve", lambda e: e.memset(cexp[:, 0:1], EXP_BIAS), "dve")
    t_consts = b.op("dve", lambda e: e.memset(cexp[:, 1:2], QK_SCALE), "dve")
    t_ms = b.op("gp", lambda e: e.memset(ident, 0.0), "gp_e")
    b.wait("gp", "gp_e", t_ms)
    b.op("gp", lambda e: e.affine_select(out=ident, in_=ident,
         compare_op=OP.not_equal, fill=1.0, base=0, pattern=[[-1, 128]],
         channel_multiplier=1), "gp_e")
    for idt in (id64b, id64f):
        t_ms = b.op("gp", lambda e, t=idt: e.memset(t, 0.0), "gp_e")
        b.wait("gp", "gp_e", t_ms)
        b.op("gp", lambda e, t=idt: e.affine_select(out=t, in_=t,
             compare_op=OP.not_equal, fill=1.0, base=0, pattern=[[-1, 64]],
             channel_multiplier=1), "gp_e")
    b.op("dve", lambda e: e.memset(v2r[:, :, :, 128:129], 1.0), "dve")

    t = b.dma("gp", c_sb, c_h.rearrange("a (k p) -> (a p) k", p=128), slow_ok=True)
    b.wait_tok("act", t)
    b.wait("act", "dve", t_consts)
    b.op("act", lambda e: e.activation(silu_c, c_sb, AF.Silu), "act")
    t_siluc = b.op("act", lambda e: e.activation(silu_cb, c_sb, AF.Silu), "act")

    slab_tok = {}
    slab_mm = {}

    def ring_slot(idx):
        return R2[:, (idx % 2) * 2048:(idx % 2) * 2048 + 2048]

    def mod_slab_dma(idx):
        if idx >= 2:
            b.wait("gp", "pe", slab_mm[idx - 2])
        g, kc = divmod(idx, 16)
        slab_tok[idx] = b.dma("gp", ring_slot(idx),
                              wmod_h[kc * 128:(kc + 1) * 128,
                                     g * 2048:(g + 1) * 2048])

    def mod_slab_mms(idx):
        g, kc = divmod(idx, 16)
        slot = ring_slot(idx)
        b.wait_tok("pe", slab_tok[idx])
        if idx == 0:
            b.wait("pe", "act", t_siluc)
        for q in range(4):
            if kc == 0:
                b.bank_acquire(q)
            b.op("pe", lambda e, o=bank(q)[0:1, :], l=silu_cb[:, kc:kc + 1],
                 r=slot[:, q * 512:(q + 1) * 512], s=(kc == 0), p=(kc == 15):
                 e.matmul(o, l, r, start=s, stop=p), "pe")
            if kc == 15:
                b.bank_written(q)
        slab_mm[idx] = b.cnt["pe"]

    bounce = [fv(R2, 4096 + 1024 * j, 512)[0:1, :] for j in range(4)]

    def repl_from_banks(dst_ap, plus1, pe_guard, dma_guard=None):
        if pe_guard:
            b.wait("dve", "pe", pe_guard)
        if dma_guard is not None:
            b.wait_tok("dve", dma_guard)
        cp_t = [0] * 4
        for q in range(4):
            b.bank_read("dve", q)
            cp_t[q] = b.op("dve", lambda e, d=bounce[q], s=bank(q)[0:1, :]:
                           e.tensor_copy(d, s), "dve")
            b.bank_read_done("dve", q)
        for q in range(4):
            bk = 6 + q % 2
            b.bank_acquire(bk)
            b.wait("pe", "dve", cp_t[q])
            b.op("pe", lambda e, o=bank(bk), l=ones_f, r=bounce[q]:
                 e.matmul(o, l, r, start=True, stop=True), "pe")
            b.bank_written(bk)
            b.bank_read("dve", bk)
            if plus1:
                b.op("dve", lambda e, d=dst_ap[:, q * 512:(q + 1) * 512],
                     s=bank(bk): e.tensor_scalar(d, s, 1.0, None, OP.add), "dve")
            else:
                b.op("dve", lambda e, d=dst_ap[:, q * 512:(q + 1) * 512],
                     s=bank(bk): e.tensor_copy(d, s), "dve")
            b.bank_read_done("dve", bk)
        return b.cnt["pe"]

    def gagb_post(row, pe_guard):
        sgn_scr, sgn2, abs_scr, abs2 = bounce
        if pe_guard:
            b.wait("dve", "pe", pe_guard)
            b.wait("act", "pe", pe_guard)
        toks = []
        for q in range(4):
            b.bank_read("dve", q)
            b.bank_read("act", q)
            src = bank(q)[0:1, :]
            b.op("dve", lambda e, s=src: e.tensor_scalar(
                sgn_scr, s, 0.0, 2.0, OP.is_ge, OP.mult), "dve")
            b.op("dve", lambda e: e.tensor_scalar(
                sgn2, sgn_scr, -1.0, None, OP.add), "dve")
            b.wait("act", "dve", b.cnt["dve"])
            if q > 0:
                b.wait_tok("act", toks[q - 1])
            b.op("act", lambda e, s=src: e.activation(abs_scr, s, AF.Abs), "act")
            b.wait("dve", "act", b.cnt["act"])
            b.op("dve", lambda e: e.tensor_scalar(abs2, abs_scr, 0.25, None,
                                                  OP.max), "dve")
            b.op("dve", lambda e: e.tensor_mul(abs_scr, sgn2, abs2), "dve")
            b.wait("act", "dve", b.cnt["dve"])
            toks.append(b.dma("sc", vec_scr[row:row + 1, q * 512:(q + 1) * 512],
                              abs_scr))
            b.bank_read_done("dve", q)
            b.bank_read_done("act", q)
        return toks

    for idx in range(16):
        mod_slab_dma(idx)
        mod_slab_mms(idx)
    t_bounce_pe = repl_from_banks(sa_repA, False, 0)
    for idx in range(16, 32):
        mod_slab_dma(idx)
        mod_slab_mms(idx)
    t_bounce_pe = repl_from_banks(ca1_repA, True, t_bounce_pe)
    cast_qkv_t = [None] * 12

    def ln_group_math(st, g):
        c0, c1 = g * 4, g * 4 + 4
        sums, sqs = st[:, c0:c1], st[:, 16 + c0:16 + c1]
        m, r = st[:, 32 + c0:32 + c1], st[:, 48 + c0:48 + c1]
        sa_, sb_, sc_, sd_, se_ = (lnscr[:, 0:4], lnscr[:, 4:8], lnscr[:, 8:12],
                                   lnscr[:, 12:16], lnscr[:, 16:20])
        b.op("dve", lambda e: e.tensor_scalar(m, sums, 1.0 / D, None, OP.mult), "dve")
        b.op("dve", lambda e: e.tensor_mul(sa_, m, m), "dve")
        b.op("dve", lambda e: e.tensor_scalar(sb_, sqs, 1.0 / D, None, OP.mult), "dve")
        b.op("dve", lambda e: e.tensor_sub(sc_, sb_, sa_), "dve")
        b.op("dve", lambda e: e.tensor_scalar(sd_, sc_, 1e-5, None, OP.add), "dve")
        b.wait("act", "dve", b.cnt["dve"])
        b.op("act", lambda e: e.activation(lnsc2[:, 0:4], sd_, AF.Sqrt), "act")
        b.wait("dve", "act", b.cnt["act"])
        b.op("dve", lambda e: e.reciprocal(se_, lnsc2[:, 0:4]), "dve")
        b.op("dve", lambda e: e.tensor_mul(sa_, se_, se_), "dve")
        b.op("dve", lambda e: e.tensor_mul(sb_, sa_, sd_), "dve")
        b.op("dve", lambda e: e.tensor_scalar(sc_, sb_, -0.5, 1.5, OP.mult, OP.add), "dve")
        b.op("dve", lambda e: e.tensor_mul(r, se_, sc_), "dve")
        nmr = lnscr[:, 20:24]
        b.op("dve", lambda e: e.tensor_mul(sa_, m, r), "dve")
        return b.op("dve", lambda e: e.tensor_scalar(nmr, sa_, -1.0, None, OP.mult), "dve")

    def tm_tile_to_fm(src_tm, dst_v, tile_i, dep_dve_val, banks=(4, 5)):
        last = None
        for q in range(4):
            bk = banks[q % 2]
            b.bank_acquire(bk)
            b.wait("pe", "dve", dep_dve_val)
            for s4 in range(4):
                dc = q * 4 + s4
                b.op("pe", lambda e, o=bank4b(bk)[:, s4, :],
                     src=src_tm[:, dc * 128:(dc + 1) * 128], s=(s4 == 0),
                     p=(s4 == 3):
                     e.matmul(o, src, ident, start=s, stop=p,
                              is_transpose=True), "pe")
            b.bank_written(bk)
            b.bank_read("dve", bk)
            dst = dst_v[:, q * 4:(q + 1) * 4, tile_i * 128:(tile_i + 1) * 128]
            last = b.op("dve", lambda e, d=dst, s=bank4b(bk):
                        e.tensor_copy(d, s), "dve")
            b.bank_read_done("dve", bk)
        return last

    x_apply_t = [0] * NT
    for g in range(4):
        for i in range(g * 4, g * 4 + 4):
            slot = i % 4
            xt = xbuf[:, slot * 2048:(slot + 1) * 2048]
            if i >= 4:
                b.wait("act", "dve", x_apply_t[i - 4])
            t_x = b.dma("sc", xt, x_hbm[i * 128:(i + 1) * 128, :])
            b.wait_tok("dve", t_x)
            b.op("dve", lambda e, x=xt, o=stats[:, i:i + 1]:
                 e.reduce_sum(o, x, axis=AX), "dve")
            b.wait_tok("act", t_x)
            b.op("act", lambda e, x=xt, o=stats[:, 16 + i:17 + i]:
                 e.activation(sq_dummy, x, AF.Square, accum_out=o), "act")
        b.wait("dve", "act", b.cnt["act"])
        ln_group_math(stats, g)
        for i in range(g * 4, g * 4 + 4):
            slot = i % 4
            xt = xbuf[:, slot * 2048:(slot + 1) * 2048]
            b.op("dve", lambda e, x=xt, r=stats[:, 48 + i:49 + i],
                 nm=lnscr[:, 20 + i % 4:21 + i % 4]:
                 e.tensor_scalar(ln_u, x, r, nm, OP.mult, OP.add), "dve")
            b.op("dve", lambda e, x=xt: e.tensor_mul(x, ln_u, ca1_repA), "dve")
            t_app = b.op("dve", lambda e, x=xt: e.tensor_add(h_tm, x, sa_repA), "dve")
            x_apply_t[i] = t_app
            tm_tile_to_fm(h_tm, R1v, i, t_app)
            for s2 in range(2):
                mod_slab_dma(32 + 2 * i + s2)
                mod_slab_mms(32 + 2 * i + s2)
            if i == 7:
                toks_ga = gagb_post(0, t_bounce_pe)
    t_h_done_pe = b.cnt["pe"]
    t_h_done_dve = b.cnt["dve"]

    def cast_half(hh):
        cast_qkv_t[hh] = b.dma("gp", wqkv_bf[:, hh * 512:(hh + 1) * 512],
                               wqkv_h[:, hh * 512:(hh + 1) * 512])
    t_bounce_pe = repl_from_banks(sb_rep, False, t_bounce_pe,
                                  dma_guard=toks_ga[3])
    for idx in range(64, 80):
        mod_slab_dma(idx)
        mod_slab_mms(idx)
        if idx == 71:
            cast_half(0)
            cast_half(4)
    t_bounce_pe = repl_from_banks(cb1_rep, True, t_bounce_pe)
    for idx in range(80, 96):
        mod_slab_dma(idx)
        mod_slab_mms(idx)
        if idx == 87:
            cast_half(8)
    toks_gb = gagb_post(1, t_bounce_pe)
    b.wait_toks("dve", toks_gb)
    for ch_half in (1, 5, 9, 2, 6, 10, 3, 7, 11):
        cast_half(ch_half)
    cast_wout_t = [b.dma("gp", wout_bf[:, ch * 1024:(ch + 1) * 1024],
                         wout_h[:, ch * 1024:(ch + 1) * 1024]) for ch in range(2)]

    cast_t = {}
    for ch in range(16):
        cast_t[("wg", ch)] = b.dma("gp", wg_bf[ch * 512:(ch + 1) * 512, :],
                                   wg_h[ch * 512:(ch + 1) * 512, :])
    for ch in range(8):
        cast_t[("wd", ch)] = b.dma("gp", wd_bf[ch * 256:(ch + 1) * 256, :],
                                   wd_h[ch * 256:(ch + 1) * 256, :])
    for ch in range(16):
        cast_t[("wu", ch)] = b.dma("gp", wu_bf[ch * 512:(ch + 1) * 512, :],
                                   wu_h[ch * 512:(ch + 1) * 512, :])

    wgT_v = wgT_h.rearrange("(ic p dc) j -> ic p dc j", ic=NB, p=128)
    wuT_v = wuT_h.rearrange("(ic p dc) j -> ic p dc j", ic=NB, p=128)
    wdT_v = wdT_h.rearrange("(g c p s) j -> g c p s j", g=4, c=NB, p=128)
    prep_t = {}
    prep_gu_toks = []
    prep_wd_toks = []
    prep_last = [0]

    def gu_prep_step(s):
        name = "wg" if s < 64 else "wu"
        dcq, qu = (s % 64) // 4, s % 4
        src_bf = wg_bf if s < 64 else wu_bf
        dstv = wgT_v if s < 64 else wuT_v
        need_ch = ((qu + 1) * 2048 - 1) // 512
        b.wait_toks("sp", [cast_t[(name, ch)] for ch in range(need_ch + 1)])
        b.wait_tok("sp", prep_last[0])
        t_tp = b.dma("sp", prep_buf,
                     src_bf[qu * 2048:(qu + 1) * 2048,
                            dcq * 128:(dcq + 1) * 128],
                     transpose=True)
        b.wait_tok("sp", t_tp)
        dst = dstv[qu * 16:(qu + 1) * 16, :, dcq, :].transpose([1, 0, 2])
        src_ap = prep_buf.rearrange("p (ic j) -> p ic j", ic=16)
        prep_last[0] = b.dma("sp", dst, src_ap)
        prep_gu_toks.append(prep_last[0])
        prep_t[name] = prep_last[0]

    def wd_prep_step(cq):
        if cq == 0:
            b.wait_toks("sp", [cast_t[("wd", ch)] for ch in range(8)])
        b.wait_tok("sp", prep_last[0])
        t_tp = b.dma("sp", prep_buf, wd_bf[:, cq * 128:(cq + 1) * 128],
                     transpose=True)
        b.wait_tok("sp", t_tp)
        dst = wdT_v[:, cq, :, :, :].transpose([1, 0, 2, 3])
        src_ap = prep_buf.rearrange("p (g s j) -> p g s j", g=4, s=4)
        prep_last[0] = b.dma("sp", dst, src_ap)
        prep_wd_toks.append(prep_last[0])
        prep_t["wd"] = prep_last[0]

    def emit_head(h):
        grp = h // 2
        if h % 2 == 0:
            cols0 = 4096 + grp * 256
            src = wqkv_bf[:, cols0:cols0 + 256].rearrange("(dc p) j -> p dc j", p=128)
            b.wait_tok("act", cast_qkv_t[cols0 // 512])
            b.wait("act", "pe", b.cnt["pe"] if grp > 0 else t_h_done_pe)
            t_wv = b.dma("sc", wv_t.rearrange("p (dc j) -> p dc j", dc=16), src)
            wv_v = wv_t.rearrange("p (dc j) -> p dc j", dc=16)
            for tt in range(16):
                bk = 2 + tt % 2
                b.bank_acquire(bk)
                b.wait_tok("pe", t_wv)
                b.wait("pe", "dve", t_h_done_dve)
                for dc in range(NDC):
                    b.op("pe", lambda e, o=bank(bk)[:, 0:256],
                         l=R1v[:, dc, tt * 128:(tt + 1) * 128],
                         r=wv_v[:, dc, :], s=(dc == 0), p=(dc == 15):
                         e.matmul(o, l, r, start=s, stop=p), "pe")
                b.bank_written(bk)
                b.bank_read("act", bk)
                b.op("act", lambda e, d=v2r[:, tt, :, 0:128],
                     s=bank(bk)[:, 0:256].rearrange("p (a c) -> p a c", a=2):
                     e.activation(d, s, AF.Copy), "act")
                b.bank_read_done("act", bk)

        for wt, dst_sb, col0 in ((wq_t, q_sb, h * 128), (wk_t, k_sb, 2048 + h * 128)):
            src = wqkv_bf[:, col0:col0 + 128].rearrange("(dc p) j -> p dc j", p=128)
            b.wait_tok("act", cast_qkv_t[col0 // 512])
            b.wait("act", "pe", b.cnt["pe"] if h > 0 else t_h_done_pe)
            t_w = b.dma("sc", wt.rearrange("p (dc j) -> p dc j", dc=16), src)
            wv_ = wt.rearrange("p (dc j) -> p dc j", dc=16)
            for n in range(4):
                bk = 2 + n % 2
                b.bank_acquire(bk)
                b.wait_tok("pe", t_w)
                b.wait("pe", "dve", t_h_done_dve)
                for dc in range(NDC):
                    b.op("pe", lambda e, o=bank(bk), l=wv_[:, dc, :],
                         r=R1v[:, dc, n * 512:(n + 1) * 512],
                         s=(dc == 0), p=(dc == 15):
                         e.matmul(o, l, r, start=s, stop=p), "pe")
                b.bank_written(bk)
                b.bank_read("act", bk)
                b.op("act", lambda e, d=dst_sb[:, n * 512:(n + 1) * 512],
                     s=bank(bk): e.activation(d, s, AF.Copy), "act")
                b.bank_read_done("act", bk)
        t_qk_evac = b.cnt["act"]

        for sq in range(4):
            pv_banks = (0, 1, 6, 7)
            for st in range(4):
                b.bank_acquire(pv_banks[st])
            for half in range(2):
                pv_before = b.cnt["pe"]
                exp_t = {}
                for tp in range(4):
                    bk0 = 2 + (2 * tp) % 4
                    for s2 in range(2):
                        tt = half * 8 + tp * 2 + s2
                        bk = bk0 + s2
                        b.bank_acquire(bk)
                        b.wait("pe", "act", t_qk_evac)
                        b.op("pe", lambda e, o=bank(bk),
                             l=k_sb[:, tt * 128:(tt + 1) * 128],
                             r=q_sb[:, sq * 512:(sq + 1) * 512]:
                             e.matmul(o, l, r, start=True, stop=True), "pe")
                        b.bank_written(bk)
                    b.bank_read("act", bk0)
                    b.bank_read("act", bk0 + 1)
                    if half == 1:
                        b.wait("act", "pe", pv_before)
                    exp_t[tp] = b.op("act", lambda e,
                                     d=PT[:, tp * 1024:(tp + 1) * 1024],
                                     s=PS[:, 512 * bk0:512 * (bk0 + 2)]:
                                     e.activation(d, s, AF.Exp,
                                                  bias=cexp[:, 0:1],
                                                  scale=cexp[:, 1:2]), "act")
                    b.bank_read_done("act", bk0)
                    b.bank_read_done("act", bk0 + 1)
                for tti in range(8):
                    tt = half * 8 + tti
                    b.wait("pe", "act", exp_t[tti // 2])
                    for st in range(4):
                        bk = pv_banks[st]
                        b.op("pe", lambda e, o=bank(bk)[:, 0:129],
                             l=PT[:, tti * 512 + st * 128:tti * 512 + (st + 1) * 128],
                             r=v2r[:, tt, h % 2, :],
                             s=(tt == 0), p=(tt == 15):
                             e.matmul(o, l, r, start=s, stop=p), "pe")
                        if tt == 15:
                            b.bank_written(bk)
            for st in range(4):
                bk = pv_banks[st]
                b.bank_read("dve", bk)
                b.op("dve", lambda e, o=inv4[:, st:st + 1],
                     s=bank(bk)[:, 128:129]: e.reciprocal(o, s), "dve")
                b.op("dve", lambda e, d=R2v[:, sq * 4 + st, h * 128:(h + 1) * 128],
                     s=bank(bk)[:, 0:128], sc=inv4[:, st:st + 1]:
                     e.tensor_scalar(d, s, sc, None, OP.mult), "dve")
                b.bank_read_done("dve", bk)

    if phases >= 2:
        gu_k = 0
        wd_k = 0
        for h in range(H):
            emit_head(h)
            if h >= 2:
                while gu_k < min(128, (h - 1) * 12):
                    gu_prep_step(gu_k)
                    gu_k += 1
            if h >= 12:
                while wd_k < min(NB, (h - 11) * 16):
                    wd_prep_step(wd_k)
                    wd_k += 1
        while gu_k < 128:
            gu_prep_step(gu_k)
            gu_k += 1
        while wd_k < NB:
            wd_prep_step(wd_k)
            wd_k += 1
        b.wait_toks("sp", toks_ga)
        t_gacols = b.dma("sp", ga_cols, vec_scr[0:1, :].rearrange(
            "a (k p) -> (a p) k", p=128), slow_ok=True)
        b.wait_toks("sp", toks_gb)
        t_gbcols = b.dma("sp", gb_cols,
                         vec_scr[1:2, :].rearrange("a (k p) -> (a p) k", p=128),
                         slow_ok=True)
        t_attn_dve = b.cnt["dve"]

    if phases >= 3:
        for i in range(NT):
            tm_tile_to_fm(R2v[:, i, :], R1v, i, t_attn_dve, banks=(4, 5))
        t_comb_dve = b.cnt["dve"]

        wo_parity_t = [0, 0]
        t_tr_prev = 0
        for dc in range(16):
            pp = dc % 2
            wo_buf = wo_sb[:, pp * 2048:(pp + 1) * 2048]
            src = wout_bf[:, dc * 128:(dc + 1) * 128].rearrange(
                "(dcin p) j -> p dcin j", p=128)
            b.wait_tok("act", cast_wout_t[dc // 8])
            b.wait("act", "pe", wo_parity_t[pp] if dc >= 2 else b.cnt["pe"])
            t_wo = b.dma("sc", wo_buf.rearrange("p (dcin j) -> p dcin j", dcin=16), src)
            wo_v = wo_buf.rearrange("p (dcin j) -> p dcin j", dcin=16)
            for n in range(4):
                bk = 2 + n % 2
                b.bank_acquire(bk)
                b.wait_tok("pe", t_wo)
                b.wait("pe", "dve", t_comb_dve)
                for dcin in range(16):
                    b.op("pe", lambda e, o=bank(bk), l=wo_v[:, dcin, :],
                         r=R1v[:, dcin, n * 512:(n + 1) * 512],
                         s=(dcin == 0), p=(dcin == 15):
                         e.matmul(o, l, r, start=s, stop=p), "pe")
                b.bank_written(bk)
                wo_parity_t[pp] = b.cnt["pe"]
                b.bank_read("act", bk)
                b.wait("act", "pe", t_tr_prev)
                b.wait_tok("act", t_gacols)
                t_ev = b.op("act", lambda e, s=bank(bk), sc=ga_cols[:, dc:dc + 1]:
                            e.activation(dout_tile, s, AF.Copy, scale=sc), "act")
                b.bank_read_done("act", bk)
                tb = 4 + n % 2
                b.bank_acquire(tb)
                b.wait("pe", "act", t_ev)
                for s4 in range(4):
                    b.op("pe", lambda e, o=bank4b(tb)[:, s4, :],
                         src=dout_tile[:, s4 * 128:(s4 + 1) * 128], s=(s4 == 0),
                         p=(s4 == 3):
                         e.matmul(o, src, ident, start=s, stop=p,
                              is_transpose=True), "pe")
                b.bank_written(tb)
                t_tr_prev = b.cnt["pe"]
                b.bank_read("dve", tb)
                dst = R2v[:, n * 4:(n + 1) * 4, dc * 128:(dc + 1) * 128]
                b.op("dve", lambda e, d=dst, s=bank4b(tb): e.tensor_copy(d, s), "dve")
                b.bank_read_done("dve", tb)

        adds_t = [0] * NT
        for i in range(NT):
            pp = i % 2
            if i >= 2:
                b.wait("act", "dve", adds_t[i - 2])
            t_x = b.dma("sc", xre[pp], x_hbm[i * 128:(i + 1) * 128, :])
            b.wait_tok("dve", t_x)
            x1t = R2v[:, i, :]
            b.op("dve", lambda e, d=x1_scr, x1=x1t, s=xre[pp]:
                 e.tensor_add(d, x1, s), "dve")
            adds_t[i] = b.op("dve", lambda e, d=x1t:
                             e.tensor_copy(d, x1_scr), "dve")
            b.op("dve", lambda e, x=x1t, o=stats2[:, i:i + 1]:
                 e.reduce_sum(o, x, axis=AX), "dve")
            b.wait("act", "dve", adds_t[i])
            b.op("act", lambda e, x=x1t, o=stats2[:, 16 + i:17 + i]:
                 e.activation(sq_d2, x, AF.Square, accum_out=o), "act")
        b.wait("dve", "act", b.cnt["act"])
        for g in range(4):
            ln_group_math(stats2, g)
        b.op("dve", lambda e: e.tensor_mul(lnscr[:, 0:16], stats2[:, 32:48],
                                           stats2[:, 48:64]), "dve")
        b.op("dve", lambda e: e.tensor_scalar(nmr2, lnscr[:, 0:16], -1.0,
                                              None, OP.mult), "dve")

    if phases >= 4:
        b.wait("sp", "act", b.cnt["act"])
        b.wait("sp", "pe", b.cnt["pe"])
        b.wait("sp", "dve", b.cnt["dve"])
        t_r1 = b.dma("sp", r1_f32, r1_h)
        b.wait_tok("dve", t_r1)
        r1tm_hi, r1tm_lo = q_sb, k_sb
        b.op("dve", lambda e: e.tensor_copy(r1tm_hi, r1_f32), "dve")
        b.op("dve", lambda e: e.tensor_sub(r1tm_lo, r1_f32, r1tm_hi), "dve")
        t_r1tm = b.cnt["dve"]
        for src_tm, dst_fm in ((r1tm_hi, r1T_hi), (r1tm_lo, r1T_lo)):
            dfm = dst_fm.rearrange("p (c j) -> p c j", c=16)
            for q in range(4):
                bk = 2 + q % 2
                b.bank_acquire(bk)
                b.wait("pe", "dve", t_r1tm)
                for s4 in range(4):
                    dc = q * 4 + s4
                    b.op("pe", lambda e, o=bank4b(bk)[:, s4, :],
                         src=src_tm[:, dc * 128:(dc + 1) * 128], s=(s4 == 0),
                         p=(s4 == 3):
                         e.matmul(o, src, ident, start=s, stop=p,
                              is_transpose=True), "pe")
                b.bank_written(bk)
                b.bank_read("dve", bk)
                b.op("dve", lambda e, d=dfm[:, q * 4:(q + 1) * 4, :],
                     s=bank4b(bk): e.tensor_copy(d, s), "dve")
                b.bank_read_done("dve", bk)
        t_r2 = b.dma("sp", r2_sb, r2_h)
        bk = 2
        b.bank_acquire(bk)
        b.wait_tok("pe", t_r2)
        b.op("pe", lambda e, o=bank(bk)[:, 0:64]: e.transpose(o, r2_sb, id64f), "pe")
        b.bank_written(bk)
        b.bank_read("dve", bk)
        b.op("dve", lambda e, s=bank(bk)[:, 0:64]: e.tensor_copy(r2T, s), "dve")
        b.bank_read_done("dve", bk)

        b.wait_toks("dve", prep_gu_toks)
        b.wait_toks("dve", prep_wd_toks)
        b.wait_toks("act", prep_gu_toks)
        b.wait_toks("act", prep_wd_toks)
        h2v_hi = h2_hi.rearrange("p (c j) -> p c j", c=16)
        h2v_lo = h2_lo.rearrange("p (c j) -> p c j", c=16)
        r1v_hi = r1T_hi.rearrange("p (c j) -> p c j", c=16)
        r1v_lo = r1T_lo.rearrange("p (c j) -> p c j", c=16)

        for s_ in range(NSTRIP):
            for k in range(4):
                i = s_ * 4 + k
                x1t = R2v[:, i, :]
                for hf in range(2):
                    xs = x1t[:, hf * 1024:(hf + 1) * 1024]
                    b.op("dve", lambda e, x=xs, r=stats2[:, 48 + i:49 + i],
                         nm=nmr2[:, i:i + 1]:
                         e.tensor_scalar(f32buf, x, r, nm, OP.mult, OP.add), "dve")
                    b.op("dve", lambda e, c=cb1_rep[:, hf * 1024:(hf + 1) * 1024]:
                         e.tensor_mul(f32buf_b, f32buf, c), "dve")
                    b.op("dve", lambda e, sbr=sb_rep[:, hf * 1024:(hf + 1) * 1024]:
                         e.tensor_add(f32buf, f32buf_b, sbr), "dve")
                    b.op("dve", lambda e: e.tensor_copy(hh, f32buf), "dve")
                    t_half = b.op("dve", lambda e: e.tensor_sub(hl, f32buf, hh), "dve")
                    for part, src in ((0, hh), (1, hl)):
                        dstv = h2v_hi if part == 0 else h2v_lo
                        for q in range(2):
                            bk = 2 + q % 2
                            b.bank_acquire(bk)
                            b.wait("pe", "dve", t_half)
                            for s4 in range(4):
                                b.op("pe", lambda e, o=bank4b(bk)[:, s4, :],
                                     src=src[:, (q * 4 + s4) * 128:
                                             (q * 4 + s4 + 1) * 128],
                                     s=(s4 == 0), p=(s4 == 3):
                                     e.matmul(o, src, ident, start=s, stop=p,
                                              is_transpose=True), "pe")
                            b.bank_written(bk)
                            b.bank_read("dve", bk)
                            dst = dstv[:, hf * 8 + q * 4:hf * 8 + (q + 1) * 4,
                                       k * 128:(k + 1) * 128]
                            b.op("dve", lambda e, d=dst, s=bank4b(bk):
                                 e.tensor_copy(d, s), "dve")
                            b.bank_read_done("dve", bk)
            t_h2 = b.cnt["dve"]

            bk = 0
            b.bank_acquire(bk)
            b.wait("pe", "dve", t_h2)
            combos = [(r1v_hi, h2v_hi), (r1v_lo, h2v_hi), (r1v_hi, h2v_lo)]
            nmm = 0
            for (wv_, hv_) in combos:
                for dc in range(16):
                    nmm += 1
                    b.op("pe", lambda e, o=bank(bk), l=wv_[:, dc, :],
                         r=hv_[:, dc, :], s=(nmm == 1), p=(nmm == 48):
                         e.matmul(o, l, r, start=s, stop=p), "pe")
            b.bank_written(bk)
            b.bank_read("act", bk)
            t_rr = b.op("act", lambda e, s=bank(bk): e.activation(rrbuf, s, AF.Silu), "act")
            b.bank_read_done("act", bk)
            for tt in range(4):
                bk = 1
                b.bank_acquire(bk)
                b.wait("pe", "act", t_rr)
                b.op("pe", lambda e, o=bank(bk)[:, 0:64],
                     l=rrbuf[:, tt * 128:(tt + 1) * 128],
                     r=r2T:
                     e.matmul(o, l, r, start=True, stop=True), "pe")
                b.bank_written(bk)
                b.bank_read("dve", bk)
                b.op("dve", lambda e, d=logits[:, tt * 64:(tt + 1) * 64],
                     s=bank(bk)[:, 0:64]: e.tensor_copy(d, s), "dve")
                b.bank_read_done("dve", bk)
                lg = logits[:, tt * 64:(tt + 1) * 64]
                lz = logits_z[:, tt * 64:(tt + 1) * 64]
                lz2 = logits_z2[:, tt * 64:(tt + 1) * 64]
                b.op("dve", lambda e, l=lg: e.max(out=m8, in_=l), "dve")
                b.op("dve", lambda e, l=lg, z=lz: e.match_replace(
                    out=z, in_to_replace=m8, in_values=l, imm_value=-1e30), "dve")
                b.op("dve", lambda e, z=lz: e.max(out=m8, in_=z), "dve")
                b.op("dve", lambda e, z=lz, z2=lz2: e.match_replace(
                    out=z2, in_to_replace=m8, in_values=z, imm_value=-1e30), "dve")
                t_msk = b.op("dve", lambda e, l=lg, z2=lz2,
                             mm=mask_tm[:, tt * 64:(tt + 1) * 64]:
                             e.tensor_tensor(mm, l, z2, OP.not_equal), "dve")
                bkm = 2
                b.bank_acquire(bkm)
                b.wait("pe", "dve", t_msk)
                b.op("pe", lambda e, src=mask_tm[:, tt * 64:(tt + 1) * 64],
                     o=bankb(bkm)[0:64, 0:128]: e.transpose(o, src, ident), "pe")
                b.bank_written(bkm)
                b.bank_read("dve", bkm)
                b.op("dve", lambda e, d=mask_fm[:, tt * 128:(tt + 1) * 128],
                     s=bankb(bkm)[0:64, 0:128]: e.tensor_copy(d, s), "dve")
                b.bank_read_done("dve", bkm)
            t_mask = b.cnt["dve"]

            wring_slots = [wring[:, j * 2048:(j + 1) * 2048] for j in range(4)]
            slot_mm_t = [0, 0, 0, 0]
            mrow_t = [0, 0, 0]
            t_si_rd = 0
            b.wait_toks("sp", prep_gu_toks)

            def mask_extract(ic):
                par = ic % 2
                mb_ = 2 if par == 0 else 5
                mslot = ic % 3
                b.bank_acquire(mb_)
                b.wait("pe", "dve", t_mask)
                b.wait("pe", "dve", mrow_t[mslot])
                b.op("pe", lambda e, o=bank(mb_)[0:1, 0:512],
                     l=id64b[:, ic:ic + 1], r=mask_fm:
                     e.matmul(o, l, r, start=True, stop=True), "pe")
                b.bank_written(mb_)
                b.bank_read("dve", mb_)
                t_mr = b.op("dve", lambda e,
                            d=mrow[0:1, mslot * 512:(mslot + 1) * 512],
                            s=bank(mb_)[0:1, 0:512]: e.tensor_copy(d, s), "dve")
                b.bank_read_done("dve", mb_)
                mrow_t[mslot] = t_mr

            mask_extract(0)
            for ic in range(NB):
                par = ic % 2
                gb_, ub_, mb_ = (0, 1, 2) if par == 0 else (3, 4, 5)
                gslot, uslot = wring_slots[2 * par], wring_slots[2 * par + 1]
                b.wait("sp", "dve", t_mask)
                if ic >= 2:
                    b.wait("sp", "pe", slot_mm_t[2 * par])
                t_wg = b.dma("sp", gslot.rearrange("p (dc j) -> p dc j", dc=16),
                             wgT_v[ic])
                t_wu = b.dma("sp", uslot.rearrange("p (dc j) -> p dc j", dc=16),
                             wuT_v[ic])
                gsv = gslot.rearrange("p (dc j) -> p dc j", dc=16)
                usv = uslot.rearrange("p (dc j) -> p dc j", dc=16)
                mslot = ic % 3
                b.bank_acquire(mb_)
                b.wait("pe", "dve", mrow_t[mslot])
                b.op("pe", lambda e, o=bank(mb_), l=ones_bf,
                     r=mrow[0:1, mslot * 512:(mslot + 1) * 512]:
                     e.matmul(o, l, r, start=True, stop=True), "pe")
                b.bank_written(mb_)
                if ic + 1 < NB:
                    mask_extract(ic + 1)
                b.bank_acquire(gb_)
                b.wait_tok("pe", t_wg)
                for dc in range(16):
                    b.op("pe", lambda e, o=bank(gb_), l=gsv[:, dc, :],
                         r=h2v_hi[:, dc, :], s=(dc == 0), p=(dc == 15):
                         e.matmul(o, l, r, start=s, stop=p), "pe")
                b.bank_written(gb_)
                b.bank_acquire(ub_)
                b.wait_tok("pe", t_wu)
                for dc in range(16):
                    b.op("pe", lambda e, o=bank(ub_), l=usv[:, dc, :],
                         r=h2v_hi[:, dc, :], s=(dc == 0), p=(dc == 15):
                         e.matmul(o, l, r, start=s, stop=p), "pe")
                b.bank_written(ub_)
                slot_mm_t[2 * par] = b.cnt["pe"]
                slot_mm_t[2 * par + 1] = b.cnt["pe"]
                b.bank_read("act", gb_)
                b.wait("act", "dve", t_si_rd)
                t_si = b.op("act", lambda e, s=bank(gb_):
                            e.activation(silu_sb, s, AF.Silu), "act")
                b.bank_read_done("act", gb_)
                b.bank_read("dve", ub_)
                b.wait("dve", "act", t_si)
                b.op("dve", lambda e, s=bank(ub_):
                     e.tensor_mul(dout_tile, silu_sb, s), "dve")
                t_si_rd = b.cnt["dve"]
                b.bank_read_done("dve", ub_)
                b.bank_read("dve", mb_)
                b.op("dve", lambda e, d=hidv[:, ic, :], s=bank(mb_):
                     e.tensor_mul(d, dout_tile, s), "dve")
                b.bank_read_done("dve", mb_)
            t_hidden = b.cnt["dve"]

            dhalf = [dring[:, 0:4096], dring[:, 4096:8192]]
            dhalf_mm_t = [0, 0]
            t_tr_prev2 = 0
            b.wait_toks("sp", prep_wd_toks)
            for g in range(4):
                banks_g = (0, 1, 2, 3) if g % 2 == 0 else (4, 5, 6, 7)
                for sub in range(4):
                    b.bank_acquire(banks_g[sub])
                for blk8 in range(8):
                    hb = blk8 % 2
                    b.wait("sp", "pe", dhalf_mm_t[hb])
                    if g == 0 and blk8 < 2:
                        b.wait("sp", "dve", t_hidden)
                    src_ap = wdT_v[g, blk8 * 8:(blk8 + 1) * 8].transpose([1, 0, 2, 3])
                    dst_ap = dhalf[hb].rearrange("p (c s j) -> p c s j", c=8, s=4)
                    t_wd = b.dma("sp", dst_ap, src_ap)
                    dsv = dst_ap
                    b.wait_tok("pe", t_wd)
                    b.wait("pe", "dve", t_hidden)
                    for ci in range(8):
                        cq = blk8 * 8 + ci
                        for sub in range(4):
                            b.op("pe", lambda e, o=bank(banks_g[sub]),
                                 l=dsv[:, ci, sub, :],
                                 r=hidv[:, cq, :], s=(cq == 0), p=(cq == NB - 1):
                                 e.matmul(o, l, r, start=s, stop=p), "pe")
                    dhalf_mm_t[hb] = b.cnt["pe"]
                for sub in range(4):
                    b.bank_written(banks_g[sub])
                for sub in range(4):
                    dc = 4 * g + sub
                    bkd = banks_g[sub]
                    tb = 4 if g % 2 == 0 else 0
                    b.bank_read("act", bkd)
                    b.wait("act", "pe", t_tr_prev2)
                    if s_ == 0 and g == 0 and sub == 0:
                        b.wait_tok("act", t_gbcols)
                    t_ev = b.op("act", lambda e, s=bank(bkd),
                                sc=gb_cols[:, dc:dc + 1]:
                                e.activation(dout_tile, s, AF.Copy, scale=sc),
                                "act")
                    b.bank_read_done("act", bkd)
                    b.bank_acquire(tb)
                    b.wait("pe", "act", t_ev)
                    for s4 in range(4):
                        b.op("pe", lambda e, o=bank4b(tb)[:, s4, :],
                             src=dout_tile[:, s4 * 128:(s4 + 1) * 128],
                             s=(s4 == 0), p=(s4 == 3):
                             e.matmul(o, src, ident, start=s, stop=p,
                                      is_transpose=True), "pe")
                    b.bank_written(tb)
                    t_tr_prev2 = b.cnt["pe"]
                    b.bank_read("dve", tb)
                    dst = out_stage.rearrange("p (t j) -> p t j", t=4)[
                        :, :, sub * 128:(sub + 1) * 128]
                    x1s = R2v[:, s_ * 4:(s_ + 1) * 4, dc * 128:(dc + 1) * 128]
                    b.op("dve", lambda e, d=dst, s=bank4b(tb), x1=x1s:
                         e.tensor_add(d, s, x1), "dve")
                    b.bank_read_done("dve", tb)
                b.wait("act", "dve", b.cnt["dve"])
                store_toks = []
                for tt in range(4):
                    row0 = s_ * 512 + tt * 128
                    store_toks.append(
                        b.dma("sc", y_hbm[row0:row0 + 128, g * 512:(g + 1) * 512],
                              out_stage[:, tt * 512:(tt + 1) * 512]))
                for tk in store_toks:
                    b.wait_tok("dve", tk)

    tap_src = {"h_fm": R1, "R1": R1, "R2": R2, "A": A, "stats": stats,
               "stats2": stats2, "logits": logits, "mask_tm": mask_tm,
               "q_sb": q_sb, "k_sb": k_sb, "v2": v2_sb, "PT": PT}
    for tname, shape, dt in taps:
        b.wait_all("act")
        src = tap_src[tname]
        b.dma("sc", tap_aps[tname], src[0:shape[0], 0:shape[1]])

    b.wait_all("gp")
    b.wait_all("sp")

    import contextlib
    with contextlib.ExitStack() as ctx:
        for s in SEMS:
            b.sems[s] = ctx.enter_context(nc.semaphore(s))
        with nc.Block() as block:
            @block.tensor
            def _(e):
                for f in b.plan["pe"]:
                    f(e)

            @block.scalar
            def _(e):
                for f in b.plan["act"]:
                    f(e)

            @block.vector
            def _(e):
                for f in b.plan["dve"]:
                    f(e)

            @block.sync
            def _(e):
                for f in b.plan["sp"]:
                    f(e)

            @block.gpsimd
            def _(e):
                for f in b.plan["gp"]:
                    f(e)
    return nc


_CACHE = {}


def _get_nc():
    if "nc" not in _CACHE:
        _CACHE["nc"] = build_nc(phases=4)
    return _CACHE["nc"]


def kernel(**inputs):
    nc = _get_nc()
    x = np.ascontiguousarray(np.asarray(inputs["L0_attn_x"], dtype=np.float32))
    shared = {
        "w_qkv": np.ascontiguousarray(inputs["L0_attn_w_qkv"], dtype=np.float32),
        "w_out": np.ascontiguousarray(inputs["L0_attn_w_out"], dtype=np.float32),
        "gate_w": np.ascontiguousarray(inputs["L0_ffn_gate_w"], dtype=np.float32),
        "up_w": np.ascontiguousarray(inputs["L0_ffn_up_w"], dtype=np.float32),
        "down_w": np.ascontiguousarray(inputs["L0_ffn_down_w"], dtype=np.float32),
        "router_w1": np.ascontiguousarray(inputs["L0_ffn_router_w1"], dtype=np.float32),
        "router_w2": np.ascontiguousarray(inputs["L0_ffn_router_w2"], dtype=np.float32),
        "wmod": np.ascontiguousarray(inputs["L0_wmod"], dtype=np.float32),
    }
    c_all = np.ascontiguousarray(np.asarray(inputs["L0_c"], dtype=np.float32))
    per_core = []
    for core in range(8):
        m = dict(shared)
        m["x"] = np.ascontiguousarray(x[core])
        m["c"] = np.ascontiguousarray(c_all[core:core + 1])
        per_core.append(m)
    res = run_bass_kernel_spmd(nc, per_core, core_ids=list(range(8)))
    return np.stack([res.results[i]["y"] for i in range(8)], axis=0)
